# version 16 (final)
# baseline (speedup 1.0000x reference)
import numpy as np
import ml_dtypes
from contextlib import ExitStack

import concourse.bass as bass
import concourse.mybir as mybir
from concourse.bass_utils import run_bass_kernel_spmd

F32 = mybir.dt.float32
BF16 = mybir.dt.bfloat16
AF = mybir.ActivationFunctionType
ALU = mybir.AluOpType
AX = mybir.AxisListType

D = 1024
FF = 4096
H = 16
LC = 2048
NCH = 16
CG = 512
NTT = LC // CG
EPS = 1e-6
EPS2_64 = 64.0 * EPS * EPS
LNEPS = 1e-5
TWO_PI = float(2.0 * np.pi)
HALF_PI = float(np.pi / 2.0)

ENGS = ("sync", "gpsimd", "tensor", "vector", "scalar")

C8L = ("a_bq_c", "a_bk_c", "a_bv_c", "a_bo_c", "a_b2_c",
       "a_ln1_g_c", "a_ln1_b_c", "a_ln2_g_c", "a_ln2_b_c",
       "f_b2_c", "b_bq_c", "b_bk_c", "b_bv_c", "b_bo_c")
C32L = ("a_b1_c", "f_b1_c", "f_qp_c", "f_qwr_c", "f_qwi_c")


class Sem:
    __slots__ = ("h", "c", "name")

    def __init__(self, h, name):
        self.h, self.c, self.name = h, 0, name


class Op:
    __slots__ = ("fn", "incs", "is_wait", "src", "kind")

    def __init__(self, fn, is_wait=False, kind=None):
        self.fn = fn
        self.incs = []
        self.is_wait = is_wait
        self.src = None
        self.kind = kind

    def inc(self, sem, n=1):
        self.incs.append((sem, n))
        sem.c += n
        return sem.c


def _ap_span(x):
    try:
        nm = x.tensor.name
        lo = x.offset
        hi = lo + 1
        for st, ct in x.ap:
            hi += abs(st) * (ct - 1)
        return (nm, lo, hi)
    except Exception:
        return None


def _overlap(span, lst):
    nm, lo, hi = span
    for (n2, l2, h2) in lst:
        if n2 == nm and lo < h2 and l2 < hi:
            return True
    return False


class ES:

    def __init__(self, prog, key):
        self._prog = prog
        self._key = key
        self._haz = key in ("vector", "scalar")
        self._w = []
        self._r = []

    def _auto_drain(self, name, a, k):
        if not self._haz:
            return
        if name == "drain":
            self._w = []
            self._r = []
            return
        aps = [s for s in (list(a) + list(k.values())) if s is not None
               for s in [_ap_span(s)] if s is not None]
        if not aps:
            return
        outs = aps[:1]
        if "accum_out" in k and k["accum_out"] is not None:
            sp = _ap_span(k["accum_out"])
            if sp:
                outs.append(sp)
        ins = aps[1:]
        conflict = any(_overlap(s, self._w) for s in ins) or                    any(_overlap(s, self._w) or _overlap(s, self._r) for s in outs)
        if conflict:
            def dfn(eng):
                return eng.drain()
            self._prog[self._key].append(Op(dfn))
            self._w = []
            self._r = []
        self._w.extend(outs)
        self._r.extend(ins)

    def __getattr__(self, name):
        if name.startswith("_"):
            raise AttributeError(name)
        def rec(*a, **k):
            self._auto_drain(name, a, k)
            def fn(eng, _n=name, _a=a, _k=k):
                return getattr(eng, _n)(*_a, **_k)
            op = Op(fn, kind=name)
            import traceback as _tb
            fr = _tb.extract_stack(limit=2)[0]
            op.src = f"{name} @ {fr.name}:{fr.lineno}"
            self._prog[self._key].append(op)
            return op
        return rec

    def wait(self, sem, v):
        assert 0 <= v <= sem.c, f"wait {sem.name}>={v} but only {sem.c} emitted"
        if v == 0:
            return
        def fn(eng):
            return eng.wait_ge(sem.h, v)
        self._prog[self._key].append(Op(fn, is_wait=True))


class B:
    def __init__(self, nc):
        self.nc = nc
        self.prog = {k: [] for k in ENGS}
        self.SY = ES(self.prog, "sync")
        self.GP = ES(self.prog, "gpsimd")
        self.TE = ES(self.prog, "tensor")
        self.VE = ES(self.prog, "vector")
        self.SE = ES(self.prog, "scalar")
        self.sems = {}
        self.t = {}

    def sem(self, ctx, name):
        s = Sem(ctx.enter_context(self.nc.semaphore(name)), name)
        self.sems[name] = s
        return s

    def replay(self):
        with self.nc.Block() as blk:
            def runner(key):
                def _(eng):
                    for op in self.prog[key]:
                        try:
                            ins = op.fn(eng)
                        except Exception:
                            print(f"REPLAY FAIL on {key} op: {getattr(op, 'src', '?')}")
                            raise
                        for (sem, n) in op.incs:
                            ins.then_inc(sem.h, n)
                return _
            blk.sync(runner("sync"))
            blk.gpsimd(runner("gpsimd"))
            blk.tensor(runner("tensor"))
            blk.vector(runner("vector"))
            blk.scalar(runner("scalar"))


def bump(b, key):
    semname = {"tensor": "te", "vector": "ve", "scalar": "se"}[key]
    sem = b.sems[semname]
    ops = b.prog[key]
    for op in reversed(ops):
        if not op.is_wait and op.kind != "dma_start":
            if not any(s is sem for s, _ in op.incs):
                op.inc(sem)
            return sem.c
    return sem.c


def sync_to(b, es, *producers):
    for p in producers:
        if p in ("tensor", "vector", "scalar"):
            v = bump(b, p)
            es.wait(b.sems[{"tensor": "te", "vector": "ve", "scalar": "se"}[p]], v)
        else:
            es.wait(b.sems[p], b.sems[p].c)


def fence(b):
    for k in ("tensor", "vector", "scalar"):
        bump(b, k)
    snap = [(k, b.sems[k].c) for k in ("te", "ve", "se", "gp", "ld")]
    for es, own in ((b.TE, "te"), (b.VE, "ve"), (b.SE, "se"), (b.GP, "gp"), (b.SY, "ld")):
        for k, v in snap:
            if k != own and v > 0:
                es.wait(b.sems[k], v)



def _col_form(v):
    v = np.asarray(v, np.float32)
    n = v.shape[0] // 128
    return np.ascontiguousarray(v.reshape(n, 128).T)


def _nb_pack(vals):
    out = np.zeros((128, 768), np.float32)
    for c in range(NCH):
        sl = vals[c * 128:(c + 1) * 128]
        out[32 * (c % 3):32 * (c % 3) + 16, 128 * (c // 3):128 * (c // 3) + 128] = sl[None, :]
    return out


def make_host_inputs(inputs, core):
    b_, r = core // 2, core % 2
    d = {}
    x = np.asarray(inputs["L0_attn_a_inputs"], np.float32)
    d["xin"] = np.ascontiguousarray(x[b_, r * LC:(r + 1) * LC, :].T)

    for nm in ("Wq", "Wk", "Wv", "Wo"):
        d[f"a_{nm}"] = np.ascontiguousarray(np.asarray(inputs[f"L0_attn_a_{nm}"], np.float32))
    for nm in ("bq", "bk", "bv", "bo"):
        d[f"a_{nm}_c"] = _col_form(inputs[f"L0_attn_a_{nm}"])
    for nm in ("bq", "bk", "bv"):
        d[f"a_{nm}_r"] = np.asarray(inputs[f"L0_attn_a_{nm}"], np.float32).reshape(1, D).copy()
    d["a_W1"] = np.ascontiguousarray(np.asarray(inputs["L0_attn_a_W1"], np.float32))
    d["a_W2"] = np.ascontiguousarray(np.asarray(inputs["L0_attn_a_W2"], np.float32))
    d["a_b1_c"] = _col_form(inputs["L0_attn_a_b1"])
    d["a_b2_c"] = _col_form(inputs["L0_attn_a_b2"])
    for nm in ("ln1_g", "ln1_b", "ln2_g", "ln2_b"):
        d[f"a_{nm}_c"] = _col_form(inputs[f"L0_attn_a_{nm}"])

    d["f_W1"] = np.ascontiguousarray(np.asarray(inputs["L0_ffn_W1"], np.float32))
    d["f_W2"] = np.ascontiguousarray(np.asarray(inputs["L0_ffn_W2"], np.float32))
    d["f_b1_c"] = _col_form(inputs["L0_ffn_b1"])
    d["f_b2_c"] = _col_form(inputs["L0_ffn_b2"])
    d["f_qp_c"] = _col_form(inputs["L0_ffn_quantum_phases"])
    d["f_qwr_c"] = _col_form(inputs["L0_ffn_quantum_weights_real"])
    d["f_qwi_c"] = _col_form(inputs["L0_ffn_quantum_weights_imag"])

    for nm in ("Wq", "Wk", "Wv"):
        w = np.asarray(inputs[f"L0_attn_b_{nm}"], np.float32)
        d[f"b_{nm}"] = np.ascontiguousarray(w.transpose(1, 0, 2).reshape(D, D))
    for nm in ("bq", "bk", "bv"):
        v = np.asarray(inputs[f"L0_attn_b_{nm}"], np.float32).reshape(-1)
        d[f"b_{nm}_c"] = _col_form(v)
    d["b_bv_r"] = np.asarray(inputs["L0_attn_b_bv"], np.float32).reshape(1, D).copy()
    d["b_Wo"] = np.ascontiguousarray(np.asarray(inputs["L0_attn_b_Wo"], np.float32))
    d["b_bo_c"] = _col_form(inputs["L0_attn_b_bo"])

    pos = np.arange(LC, dtype=np.float32) + 1.0 + r * LC
    d["n_pk"] = _nb_pack(pos)
    d["role_mask"] = np.full((128, 1), float(r), np.float32)

    qpos = np.arange(LC, dtype=np.int64) + r * LC
    kpos = np.arange(4096, dtype=np.int64)
    mb = np.zeros((4, 32, 128, 512), np.float32)
    for iqp in range(4):
        qp = qpos[iqp * 512:(iqp + 1) * 512]
        for kb in range(32):
            kp = kpos[kb * 128:(kb + 1) * 128]
            mb[iqp, kb] = (kp[:, None] <= qp[None, :]).astype(np.float32)
    d["mb"] = mb.reshape(4 * 32 * 128, 512).astype(ml_dtypes.bfloat16)
    d["cpk"] = np.concatenate([d.pop(nm) for nm in C8L + C32L], axis=1)
    return d



def declare_io(nc, t, dbg=0):
    def inp(name, shape, dtype=F32):
        t[name] = nc.dram_tensor(name, list(shape), dtype, kind="ExternalInput").ap()
    inp("xin", (D, LC))
    for nm in ("a_Wq", "a_Wk", "a_Wv", "a_Wo", "b_Wq", "b_Wk", "b_Wv", "b_Wo"):
        inp(nm, (D, D))
    inp("a_W1", (D, FF)); inp("f_W1", (D, FF))
    inp("a_W2", (FF, D)); inp("f_W2", (FF, D))
    inp("cpk", (128, 272))
    for nm in ("a_bq_r", "a_bk_r", "a_bv_r", "b_bv_r"):
        inp(nm, (1, D))
    inp("n_pk", (128, 768))
    inp("role_mask", (128, 1))
    inp("mb", (4 * 32 * 128, 512), BF16)

    t["xout"] = nc.dram_tensor("xout", [D, LC], F32, kind="ExternalOutput").ap()
    for i in range(dbg):
        t[f"dbg{i}"] = nc.dram_tensor(f"dbg{i}", [D, LC], F32, kind="ExternalOutput").ap()

    t["c_ones_col_f"] = nc.inline_tensor(np.ones((128, 1), np.float32), name="c_ones_col_f").ap()
    t["c_ones_col_bf"] = nc.inline_tensor(np.ones((128, 1), ml_dtypes.bfloat16), name="c_ones_col_bf").ap()
    t["c_ones_row_f"] = nc.inline_tensor(np.ones((1, 128), np.float32), name="c_ones_row_f").ap()
    t["c_ones_row_bf"] = nc.inline_tensor(np.ones((1, 128), ml_dtypes.bfloat16), name="c_ones_row_bf").ap()
    t["c_ident_bf"] = nc.inline_tensor(np.eye(128, dtype=ml_dtypes.bfloat16), name="c_ident_bf").ap()
    t["c_ident_f"] = nc.inline_tensor(np.eye(128, dtype=np.float32), name="c_ident_f").ap()
    u = (np.arange(128)[:, None] <= np.arange(128)[None, :]).astype(np.float32)
    t["c_umask_f"] = nc.inline_tensor(u, name="c_umask_f").ap()
    t["c_halfpi"] = nc.inline_tensor(np.full((128, 1), HALF_PI, np.float32), name="c_halfpi").ap()
    t["c_ones128_f"] = nc.inline_tensor(np.ones((128, 128), np.float32), name="c_ones128_f").ap()
    t["c_ones128_bf"] = nc.inline_tensor(np.ones((128, 128), ml_dtypes.bfloat16), name="c_ones128_bf").ap()

    for nm in ("a_Wq", "a_Wk", "a_Wv", "a_Wo", "b_Wq", "b_Wk", "b_Wv", "b_Wo"):
        t[nm + "_bf"] = nc.dram_tensor(nm + "_bf", [D, D], BF16, kind="Internal").ap()
    for nm in ("a_W1", "f_W1"):
        t[nm + "_bf"] = nc.dram_tensor(nm + "_bf", [D, FF], BF16, kind="Internal").ap()
    for nm in ("a_W2", "f_W2"):
        t[nm + "_bf"] = nc.dram_tensor(nm + "_bf", [FF, D], BF16, kind="Internal").ap()

    t["cc_a_in"] = nc.dram_tensor("cc_a_in", [128, 552], F32, kind="Internal").ap()
    t["cc_a_out"] = nc.dram_tensor("cc_a_out", [256, 552], F32, kind="Internal").ap()
    t["ccKa_in"] = nc.dram_tensor("ccKa_in", [1024, 1536], BF16, kind="Internal").ap()
    t["ccKa_out"] = nc.dram_tensor("ccKa_out", [2048, 1536], BF16, kind="Internal").ap()
    t["ccVa_in"] = nc.dram_tensor("ccVa_in", [768, 2048], BF16, kind="Internal").ap()
    t["ccVa_out"] = nc.dram_tensor("ccVa_out", [1536, 2048], BF16, kind="Internal").ap()
    t["ccKb_in"] = nc.dram_tensor("ccKb_in", [1024, 512], BF16, kind="Internal").ap()
    t["ccKb_out"] = nc.dram_tensor("ccKb_out", [2048, 512], BF16, kind="Internal").ap()
    t["ccVb_in"] = nc.dram_tensor("ccVb_in", [256, 2048], BF16, kind="Internal").ap()
    t["ccVb_out"] = nc.dram_tensor("ccVb_out", [512, 2048], BF16, kind="Internal").ap()
    t["ccN_in"] = nc.dram_tensor("ccN_in", [1, 16], BF16, kind="Internal").ap()
    t["ccN_out"] = nc.dram_tensor("ccN_out", [2, 16], BF16, kind="Internal").ap()


def alloc_sbuf(b, ctx):
    nc, t = b.nc, b.t
    def sb(name, shape, dtype=F32):
        t[name] = ctx.enter_context(nc.sbuf_tensor(name, list(shape), dtype)).ap()
    sb("X", (128, 8 * LC))
    sb("SC0", (128, 8 * CG))
    sb("SC1", (128, 2 * CG))
    sb("BC", (128, 2 * CG))
    sb("WS2", (128, 4 * 2048), BF16)
    sb("ACT", (128, 32 * CG), BF16)
    sb("QF", (128, 8 * CG), BF16)
    sb("KF", (128, 8 * CG), BF16)
    sb("XN", (128, 8 * CG), BF16)
    sb("MIX", (128, 8 * CG), BF16)
    sb("RV", (128, 16))
    sb("RVB", (128, 32), BF16)
    sb("RV2", (128, 16))
    sb("RV2B", (128, 32), BF16)
    sb("KVR", (128, 512))
    sb("KVB", (128, 1024), BF16)
    sb("RSC", (16, 8))
    sb("SAq", (128, 64))
    for nm in ("Qrk", "Qrq", "Qcrk", "Qcrq", "Qs1p", "Qs2p", "Qinv1", "Qsi", "Qso",
               "Qs3p", "Qs4p", "Qsa", "Qe", "Qccs", "Qsc", "Qsasc", "Qw1", "Qw2",
               "Qnw", "Qiw"):
        sb(nm, (16, 128))
    sb("COLS", (128, 128)); sb("COLS2", (128, 64)); sb("COLS2B", (128, 128), BF16)
    sb("JNK", (128, 16))
    sb("VSC", (128, 64), BF16)
    sb("ones_col_f", (128, 1)); sb("ones_col_bf", (128, 1), BF16)
    sb("ones_row_f", (1, 128)); sb("onesr_bf_rows", (128, 128), BF16)
    sb("ident_f", (128, 128))
    sb("umask_f", (128, 128)); sb("ones128_f", (128, 128)); sb("halfpi", (128, 1))
    sb("n_pk_s", (128, 768))
    sb("role_mask_s", (128, 1))
    sb("CPK", (128, 272))
    off = 0
    for nm in C8L + C32L:
        w = 8 if nm in C8L else 32
        t[nm + "_s"] = t["CPK"][:, off:off + w]
        off += w
    sb("BROWA", (128, 2 * D), BF16)
    sb("LNWS", (128, 512))
    sb("LNRB", (128, 1536))
    sb("f_c1", (128, 32)); sb("f_c2", (128, 32))
    sb("MQA", (1, 16)); sb("MQB", (1, 16)); sb("MQC", (1, 16)); sb("MQD", (1, 16))
    sb("MH", (128, 16))
    for i in range(8):
        t[f"p{i}"] = ctx.enter_context(nc.psum_tensor(f"p{i}", [128, 512], F32)).ap()


def AR(t):
    A = t["ACT"]
    return dict(
        AKQ=A[:, 0:2048], AQK=A[:, 2048:4096],
        QT=A[:, 4096:5120], KT=A[:, 5120:6144], VT=A[:, 6144:7168],
        MB=A[:, 0:8192],
        KS=A[:, 8192:10240], VS=A[:, 10240:12416], PT=A[:, 12416:13440],
    )



def cast_phase(b, i):
    if getattr(b, "cast_done", None) is None:
        b.cast_done = set()
    if i in b.cast_done:
        return
    b.cast_done.add(i)
    semname, names = b.cast_phases[i]
    sem = b.sems[semname]
    for nm in names:
        b.GP.dma_start(out=b.t[nm + "_bf"][:, :], in_=b.t[nm][:, :]).inc(sem, 16)


def prologue(b):
    t, SY, VE = b.t, b.SY, b.VE
    ld, ve = b.sems["ld"], b.sems["ve"]
    b.cast_phases = [("cw0", ("a_Wq", "a_Wk", "a_Wv")),
                     ("cw1", ("a_Wo", "a_W1", "a_W2")),
                     ("cw2", ("f_W1", "f_W2")),
                     ("cw3", ("b_Wk", "b_Wv", "b_Wq", "b_Wo"))]
    b.cast_ready = {}
    for semname, names in b.cast_phases:
        for nm in names:
            b.cast_ready[nm] = (b.sems[semname], 16 * len(names))
    for e in range(8):
        SY.dma_start(out=t["X"][:, e * LC:(e + 1) * LC],
                     in_=t["xin"][e * 128:(e + 1) * 128, :]).inc(ld, 16)
    pairs = [("ones_col_f", "c_ones_col_f"), ("ones_col_bf", "c_ones_col_bf"),
             ("ones_row_f", "c_ones_row_f"),
             ("ident_f", "c_ident_f"), ("onesr_bf_rows", "c_ones128_bf"),
             ("umask_f", "c_umask_f"), ("ones128_f", "c_ones128_f"), ("halfpi", "c_halfpi"),
             ("n_pk_s", "n_pk"), ("role_mask_s", "role_mask")]
    for dst, src in pairs:
        b.GP.dma_start(out=t[dst][:, :], in_=t[src][:, :]).inc(ld, 16)
    b.GP.dma_start(out=t["CPK"][:, :], in_=t["cpk"][:, :]).inc(ld, 16)
    brow_slots = {"a_bq_r": ("BROWA", 0, 0), "a_bk_r": ("BROWA", 32, 0),
                  "a_bv_r": ("BROWA", 64, 0), "b_bv_r": ("BROWA", 0, 1024)}
    b.brow_slots = brow_slots
    names4 = ("a_bq_r", "a_bk_r", "a_bv_r", "b_bv_r")
    for i, nm in enumerate(names4):
        SY.dma_start(out=t["SC1"][32 * i:32 * i + 1, 0:D], in_=t[nm][:, :]).inc(ld, 16)
    b.ld_done = ld.c
    VE.wait(ld, b.ld_done)
    for i, nm in enumerate(names4):
        tn, hb_, hc_ = brow_slots[nm]
        VE.tensor_copy(t[tn][hb_:hb_ + 1, hc_:hc_ + D], t["SC1"][32 * i:32 * i + 1, 0:D]).inc(ve)
    VE.memset(t["RV"][:, :], 0.0)
    VE.memset(t["RV2"][:, :], 0.0)
    VE.memset(t["KVR"][:, :], 0.0)
    VE.memset(t["RSC"][:, :], 0.0)
    for nm in ("Qrk", "Qrq", "Qcrk", "Qcrq", "Qs1p", "Qs2p", "Qinv1", "Qsi", "Qso",
               "Qs3p", "Qs4p", "Qsa", "Qe", "Qccs", "Qsc", "Qsasc", "Qw1", "Qw2",
               "Qnw", "Qiw"):
        VE.memset(t[nm][:, :], 0.0)
    VE.memset(t["COLS"][:, :], 0.0)
    VE.drain()
    VE.tensor_copy(t["RSC"][0:1, 0:1], t["RSC"][0:1, 0:1]).inc(ve)
    fence(b)


def quantum_consts(b):
    t, VE, SE = b.t, b.VE, b.SE
    ve, se = b.sems["ve"], b.sems["se"]
    th, b1 = t["f_qp_c_s"][:, :], t["f_b1_c_s"][:, :]
    wr, wi = t["f_qwr_c_s"][:, :], t["f_qwi_c_s"][:, :]
    sa, sbt, sd = t["SAq"][:, 0:32], t["SAq"][:, 32:64], t["COLS2"][:, 32:64]

    def rr_sin(dst):
        VE.drain()
        VE.tensor_copy(sbt, sa)
        for _ in range(9):
            VE.drain()
            VE.tensor_single_scalar(sd, sbt, -0.5, ALU.add)
            VE.drain()
            VE.tensor_single_scalar(sd, sd, 1e30, ALU.mult)
            VE.drain()
            VE.tensor_scalar_min(sd, sd, 1.0)
            VE.drain()
            VE.tensor_scalar_max(sd, sd, 0.0)
            VE.drain()
            VE.tensor_sub(sbt, sbt, sd)
            VE.drain()
            VE.tensor_single_scalar(sd, sbt, 0.5, ALU.add)
            VE.drain()
            VE.tensor_single_scalar(sd, sd, -1e30, ALU.mult)
            VE.drain()
            VE.tensor_scalar_min(sd, sd, 1.0)
            VE.drain()
            VE.tensor_scalar_max(sd, sd, 0.0)
            VE.drain()
            v = VE.tensor_add(sbt, sbt, sd).inc(ve)
        SE.wait(ve, v)
        return SE.activation(dst, sbt, AF.Sin, scale=TWO_PI)

    VE.tensor_single_scalar(sa, th, 1.0 / TWO_PI, ALU.mult)
    v = rr_sin(t["COLS"][:, 0:32]).inc(se)
    VE.wait(se, v)
    ve_spacer(b)
    VE.tensor_single_scalar(sa, th, 1.0 / TWO_PI, ALU.mult)
    VE.drain()
    VE.tensor_single_scalar(sa, sa, 0.25, ALU.add)
    VE.drain()
    v = rr_sin(t["COLS"][:, 32:64]).inc(se)
    VE.wait(se, v)
    ve_spacer(b)
    A = t["COLS"][:, 64:96]
    VE.tensor_mul(sa, t["COLS"][:, 32:64], wr)
    VE.tensor_mul(sd, t["COLS"][:, 0:32], wi)
    VE.drain()
    VE.tensor_add(sa, sa, sd)
    VE.drain()
    VE.tensor_single_scalar(sa, sa, 0.1, ALU.mult)
    VE.drain()
    VE.tensor_mul(A, sa, wi)
    VE.drain()
    VE.tensor_single_scalar(sa, th, 1.0 / TWO_PI, ALU.mult)
    VE.tensor_single_scalar(sd, b1, 0.1 / TWO_PI, ALU.mult)
    VE.drain()
    VE.tensor_add(sa, sa, sd)
    VE.drain()
    v = rr_sin(t["COLS"][:, 96:128]).inc(se)
    VE.wait(se, v)
    ve_spacer(b)
    VE.tensor_single_scalar(sa, th, 1.0 / TWO_PI, ALU.mult)
    VE.tensor_single_scalar(sd, b1, 0.1 / TWO_PI, ALU.mult)
    VE.drain()
    VE.tensor_add(sa, sa, sd)
    VE.drain()
    VE.tensor_single_scalar(sa, sa, 0.25, ALU.add)
    VE.drain()
    v = rr_sin(t["COLS2"][:, 0:32]).inc(se)
    VE.wait(se, v)
    ve_spacer(b)
    VE.tensor_mul(t["f_c1"][:, :], A, t["COLS"][:, 96:128])
    VE.tensor_mul(t["f_c2"][:, :], A, t["COLS2"][:, 0:32]).inc(ve)
    fence(b)


def ln_rows(b, src_tile, persist_base=None):
    t, TE, VE, SE = b.t, b.TE, b.VE, b.SE
    te, ve, se = b.sems["te"], b.sems["ve"], b.sems["se"]
    sync_to(b, SE, "vector")
    sync_to(b, TE, "vector")
    rel = []
    for e in range(8):
        sq = t["SC1"][:, (e % 2) * CG:(e % 2) * CG + CG]
        if e >= 2:
            SE.wait(te, rel[e - 2])
        sqv = SE.activation(sq, src_tile(e), AF.Square).inc(se)
        TE.wait(se, sqv)
        TE.matmul(t["p4"][0:1, :], t["ones_col_f"][:, :], src_tile(e), start=(e == 0), stop=(e == 7))
        rel.append(TE.matmul(t["p5"][0:1, :], t["ones_col_f"][:, :], sq, start=(e == 0), stop=(e == 7)).inc(te))
    mm_done = b.sems["te"].c
    m = t["LNWS"][0:1, :]
    r = t["LNWS"][32:33, :]
    tmp = t["LNWS"][64:65, :]
    VE.wait(te, mm_done)
    VE.tensor_single_scalar(m, t["p4"][0:1, :], 1.0 / D, ALU.mult)
    VE.drain()
    VE.tensor_mul(tmp, m, m)
    VE.drain()
    VE.scalar_tensor_tensor(r, t["p5"][0:1, :], 1.0 / D, tmp, ALU.mult, ALU.subtract)
    VE.drain()
    v = VE.tensor_single_scalar(r, r, LNEPS, ALU.add).inc(ve)
    b.bank_rel["p4"] = (b.sems["ve"], v)
    b.bank_rel["p5"] = (b.sems["ve"], v)
    SE.wait(ve, v)
    v = SE.activation(r, r, AF.Sqrt).inc(se)
    VE.wait(se, v)
    ve_spacer(b)
    out = VE.reciprocal(r, r)
    VE.drain()
    if persist_base is not None:
        rb, rc, mb2, mc2 = _lnr_slot(persist_base // 32)
        VE.tensor_copy(t["LNRB"][mb2:mb2 + 1, mc2:mc2 + 512], m)
        out = VE.tensor_copy(t["LNRB"][rb:rb + 1, rc:rc + 512], r)
    return out.inc(ve)


def ln_bcast(b, ready_ve, persist_base=None):
    t, TE, VE = b.t, b.TE, b.VE
    te, ve = b.sems["te"], b.sems["ve"]
    if persist_base is not None:
        rb, rc, mb2, mc2 = _lnr_slot(persist_base // 32)
        VE.tensor_copy(t["LNWS"][0:1, :], t["LNRB"][mb2:mb2 + 1, mc2:mc2 + 512])
        v0 = VE.tensor_copy(t["LNWS"][32:33, :], t["LNRB"][rb:rb + 1, rc:rc + 512]).inc(ve)
        ready_ve = max(ready_ve, v0)
    m = t["LNWS"][0:1, :]
    r = t["LNWS"][32:33, :]
    bank_wait(b, TE, "p6")
    bank_wait(b, TE, "p7")
    TE.wait(ve, ready_ve)
    TE.matmul(t["p6"][:, :], t["ones128_f"][32:33, :], r, start=True, stop=True)
    v = TE.matmul(t["p7"][:, :], t["ones_row_f"][:, :], m, start=True, stop=True).inc(te)
    VE.wait(te, v)
    VE.tensor_copy(t["BC"][:, 0:512], t["p6"][:, :])
    VE.drain()
    out = VE.scalar_tensor_tensor(t["BC"][:, 512:1024], t["p7"][:, :], -1.0,
                                  t["BC"][:, 0:512], ALU.mult, ALU.mult).inc(ve)
    b.bank_rel["p6"] = (b.sems["ve"], out)
    b.bank_rel["p7"] = (b.sems["ve"], out)
    return out


def apply_ln(b, src_tile, dst_tile, bc_ready, gcol=None, bcol=None):
    t, VE = b.t, b.VE
    ve = b.sems["ve"]
    VE.wait(ve, bc_ready)
    last = None
    for e0 in range(0, 8, 2):
        a_ = t["SC1"][:, 0:CG]
        b_ = t["SC1"][:, CG:2 * CG]
        VE.tensor_mul(a_, src_tile(e0), t["BC"][:, 0:512])
        VE.tensor_mul(b_, src_tile(e0 + 1), t["BC"][:, 0:512])
        if gcol is None:
            VE.tensor_add(dst_tile(e0), a_, t["BC"][:, 512:1024])
            last = VE.tensor_add(dst_tile(e0 + 1), b_, t["BC"][:, 512:1024])
        else:
            VE.tensor_add(a_, a_, t["BC"][:, 512:1024])
            VE.tensor_add(b_, b_, t["BC"][:, 512:1024])
            VE.tensor_scalar(dst_tile(e0), a_, t[gcol][:, e0:e0 + 1], t[bcol][:, e0:e0 + 1],
                             ALU.mult, ALU.add)
            last = VE.tensor_scalar(dst_tile(e0 + 1), b_, t[gcol][:, e0 + 1:e0 + 2], t[bcol][:, e0 + 1:e0 + 2],
                                    ALU.mult, ALU.add)
    return last.inc(ve)


class WStream:

    def __init__(self, b, buf_name, slot_w, nslots=2, semgroup=None):
        self.b = b
        self.buf = b.t[buf_name]
        self.slot_w = slot_w
        self.nslots = nslots
        self.n = 0
        self.loads = []
        self.rels = []
        self.ssems = [b.sems[f"{semgroup}{i}"] for i in range(nslots)]

    def load(self, dram_ap, rearr=None, inner=None, ready=None):
        b = self.b
        if ready is not None:
            b.GP.wait(ready[0], ready[1])
        par = self.n % self.nslots
        if self.n >= self.nslots:
            s, v = self.rels[self.n - self.nslots]
            assert s is not None, "WStream slot reused before release recorded"
            b.GP.wait(s, v)
        dst = self.buf[:, par * self.slot_w:(par + 1) * self.slot_w]
        if rearr is None:
            src = dram_ap
            dst_v = dst
        else:
            src = dram_ap.rearrange(rearr[0], **rearr[1])
            dst_v = dst.rearrange("p (c i) -> p c i", i=inner)
        ssem = self.ssems[par]
        v = b.GP.dma_start(out=dst_v, in_=src).inc(ssem, 16)
        self.loads.append((ssem, v))
        self.rels.append(None)
        self.n += 1
        return self.n - 1, dst

    def wait_on(self, es, idx):
        s, v = self.loads[idx]
        es.wait(s, v)

    def release(self, idx, sem, val):
        if self.rels[idx] is None or self.rels[idx][0] is not sem or self.rels[idx][1] < val:
            self.rels[idx] = (sem, val)




def sdma(b, ES_, sem, out, in_):
    ES_.wait(sem, sem.c)
    return ES_.dma_start(out=out, in_=in_).inc(sem, 16)


def _lnr_slot(tt):
    if tt < 3:
        return (32 * tt, 0, 32 * tt, 512)
    return (0, 1024, 32, 1024)


def ve_spacer(b):
    b.VE.tensor_copy(b.t["JNK"][:, :], b.t["ones128_f"][:, 0:16])


def bank_wait(b, es, bank_name):
    s = b.bank_rel.get(bank_name)
    if s is not None:
        es.wait(s[0], s[1])


def proj_feat(b, ws, wname, xn_tile, out_cb, bias_col=None, sigmoid=False,
              banks=("p0", "p1"), rhs_w=CG, widxs=None, release=True):
    t = b.t
    TE, SE = b.TE, b.SE
    te, se = b.sems["te"], b.sems["se"]
    if widxs is None:
        idxs = []
        for q4 in range(4):
            idxs.append(ws.load(t[wname + "_bf"][:, q4 * 256:(q4 + 1) * 256],
                                rearr=("(c p) f -> p c f", dict(p=128)), inner=256,
                                ready=b.cast_ready[wname]))
    else:
        idxs = widxs
    for ob in range(8):
        q4, obl = ob // 2, ob % 2
        idx, wdst = idxs[q4]
        bname = banks[ob % 2]
        bank = t[bname]
        bank_wait(b, TE, bname)
        ws.wait_on(TE, idx)
        for e in range(8):
            lhs = wdst[:, e * 256 + obl * 128: e * 256 + obl * 128 + 128]
            op = TE.matmul(bank[:, 0:rhs_w], lhs, xn_tile(e), start=(e == 0), stop=(e == 7))
        mmv = op.inc(te)
        if ob % 2 == 1 and release:
            ws.release(idx, b.sems["te"], mmv)
        if callable(out_cb):
            out_cb(ob, bank, mmv)
        else:
            tile_fn = out_cb[0]
            SE.wait(te, mmv)
            func = AF.Sigmoid if sigmoid else AF.Identity
            kw = {}
            if bias_col is not None:
                kw["bias"] = t[bias_col][:, ob:ob + 1]
            ev = SE.activation(tile_fn(ob), bank[:, 0:rhs_w], func, **kw).inc(se)
            b.bank_rel[bname] = (b.sems["se"], ev)


def proj_tok(b, whalf_dsts, whalf_idxs, ws, xn_tile, out_ap, bias_row=None,
             sigmoid=False, banks=("p0", "p1")):
    t = b.t
    TE, SE = b.TE, b.SE
    te, se = b.sems["te"], b.sems["se"]
    for c4 in range(4):
        for q4 in range(4):
            idx = whalf_idxs[q4]
            wdst = whalf_dsts[q4]
            bname = banks[(c4 * 4 + q4) % 2]
            bank = t[bname]
            bank_wait(b, TE, bname)
            ws.wait_on(TE, idx)
            for e in range(8):
                lhs = xn_tile(e)[:, c4 * 128:(c4 + 1) * 128]
                rhs = wdst[:, e * 256:(e + 1) * 256]
                op = TE.matmul(bank[:, 0:256], lhs, rhs, start=(e == 0),
                               stop=(e == 7 and bias_row is None))
            if bias_row is not None:
                tn, hb_, hc_ = b.brow_slots[bias_row]
                op = TE.matmul(bank[:, 0:256], t["onesr_bf_rows"][hb_:hb_ + 1, :],
                               t[tn][hb_:hb_ + 1, hc_ + q4 * 256:hc_ + (q4 + 1) * 256],
                               start=False, stop=True)
            mmv = op.inc(te)
            ws.release(idx, b.sems["te"], mmv)
            SE.wait(te, mmv)
            func = AF.Sigmoid if sigmoid else AF.Identity
            ev = SE.activation(out_ap[:, c4 * 1024 + q4 * 256: c4 * 1024 + (q4 + 1) * 256],
                               bank[:, 0:256], func).inc(se)
            b.bank_rel[bname] = (b.sems["se"], ev)


def emit_ffn(b, wp):
    if wp == "f":
        cast_phase(b, 2)
    t = b.t
    TE, VE, SE = b.TE, b.VE, b.SE
    te, ve, se = b.sems["te"], b.sems["ve"], b.sems["se"]
    ws2 = WStream(b, "WS2", 2048, 4, semgroup="w2s")
    rv0 = ln_rows(b, lambda e: t["X"][:, e * LC: e * LC + CG], persist_base=0)
    for tt in range(NTT):
        Xt = lambda e, tt=tt: t["X"][:, e * LC + tt * CG: e * LC + tt * CG + CG]
        XNe = lambda e: t["XN"][:, e * CG:(e + 1) * CG]
        bcv = ln_bcast(b, rv0 if tt == 0 else b.sems["ve"].c, persist_base=32 * tt)
        xnv = apply_ln(b, Xt, XNe, bcv)
        if getattr(b, "dbg_fence", 0) >= 1:
            fence(b)
        act_v = None
        act_vs = []
        for hb2 in range(16):
            idx, wdst = ws2.load(t[f"{wp}_W1_bf"][:, hb2 * 256:(hb2 + 1) * 256],
                                 rearr=("(c p) h -> p c h", dict(p=128)), inner=256,
                                 ready=b.cast_ready[f"{wp}_W1"])
            for sub in range(2):
                hb = hb2 * 2 + sub
                bname = ("p0", "p1", "p2", "p3")[hb % 4]
                bank = t[bname]
                bank_wait(b, TE, bname)
                if sub == 0:
                    ws2.wait_on(TE, idx)
                if hb == 0:
                    TE.wait(ve, xnv)
                for e in range(8):
                    op = TE.matmul(bank[:, :],
                                   wdst[:, e * 256 + sub * 128: e * 256 + sub * 128 + 128],
                                   XNe(e), start=(e == 0), stop=(e == 7))
                mmv = op.inc(te)
                if sub == 1:
                    ws2.release(idx, b.sems["te"], mmv)
                par = hb % 4
                Cs = t["SC0"][:, par * 1024: par * 1024 + 512]
                Ss = t["SC0"][:, par * 1024 + 512: par * 1024 + 1024]
                SE.wait(te, mmv)
                if hb >= 4:
                    SE.wait(ve, act_vs[hb - 4])
                SE.activation(Cs, bank[:, :], AF.Sin, scale=0.1, bias=t["halfpi"][:, 0:1])
                sev = SE.activation(Ss, bank[:, :], AF.Sin, scale=0.1).inc(se)
                VE.wait(se, sev)
                ve_spacer(b)
                u1 = t["SC1"][:, (hb % 2) * 512: (hb % 2) * 512 + 512]
                pv = VE.scalar_tensor_tensor(u1, Cs, t["f_c1"][:, hb:hb + 1], bank[:, :],
                                             ALU.mult, ALU.add).inc(ve)
                b.bank_rel[bname] = (b.sems["ve"], pv)
                VE.drain()
                VE.scalar_tensor_tensor(u1, Ss, t["f_c2"][:, hb:hb + 1], u1, ALU.mult, ALU.add)
                VE.drain()
                act_v = VE.tensor_scalar(t["ACT"][:, hb * 512:(hb + 1) * 512], u1,
                                         t[f"{wp}_b1_c_s"][:, hb:hb + 1], 0.0,
                                         ALU.add, ALU.max).inc(ve)
                act_vs.append(act_v)
        if getattr(b, "dbg_fence", 0) >= 2:
            fence(b)
        for ob in range(8):
            bname = "p2" if ob % 2 == 0 else "p3"
            bank = t[bname]
            for half in range(2):
                idx, wdst = ws2.load(
                    t[f"{wp}_W2_bf"][half * 2048:(half + 1) * 2048, ob * 128:(ob + 1) * 128],
                    rearr=("(c p) d -> p c d", dict(p=128)), inner=128,
                    ready=b.cast_ready[f"{wp}_W2"])
                if half == 0:
                    bank_wait(b, TE, bname)
                    TE.wait(ve, act_v)
                ws2.wait_on(TE, idx)
                for c16 in range(16):
                    hb = half * 16 + c16
                    op = TE.matmul(bank[:, :], wdst[:, c16 * 128:(c16 + 1) * 128],
                                   t["ACT"][:, hb * 512:(hb + 1) * 512],
                                   start=(hb == 0), stop=(hb == 31))
                mmv = op.inc(te)
                ws2.release(idx, b.sems["te"], mmv)
            VE.wait(te, mmv)
            xv = VE.scalar_tensor_tensor(Xt(ob), bank[:, :], t[f"{wp}_b2_c_s"][:, ob:ob + 1],
                                         Xt(ob), ALU.add, ALU.add).inc(ve)
            b.bank_rel[bname] = (b.sems["ve"], xv)
            if ob == 3 and tt + 1 < NTT:
                ln_rows(b, lambda e, tn=tt + 1: t["X"][:, e * LC + tn * CG: e * LC + tn * CG + CG],
                        persist_base=32 * (tt + 1))
    fence(b)


def emit_attn_b(b):
    cast_phase(b, 3)
    t = b.t
    TE, VE, SE, GP, SY = b.TE, b.VE, b.SE, b.GP, b.SY
    te, ve, se, gp, ld, cc = (b.sems[k] for k in ("te", "ve", "se", "gp", "ld", "cc"))
    A = t["ACT"]
    KSsl = [A[:, 0:256], A[:, 256:512]]
    VSsl = [A[:, 512:784], A[:, 784:1056]]
    PTsl = [A[:, 1056:1568], A[:, 1568:2080]]
    MBsl = [A[:, 2080:2592], A[:, 2592:3104]]
    ws2 = WStream(b, "WS2", 2048, 4, semgroup="w2s")

    first_norm = [True]
    rv0 = ln_rows(b, lambda e: t["X"][:, e * LC: e * LC + CG], persist_base=0)
    for tt in range(NTT):
        Xt = lambda e, tt=tt: t["X"][:, e * LC + tt * CG: e * LC + tt * CG + CG]
        XNe = lambda e: t["XN"][:, e * CG:(e + 1) * CG]
        if tt > 0:
            SE.wait(ld, ld.c)
            VE.wait(ld, ld.c)
        bcv = ln_bcast(b, rv0 if tt == 0 else b.sems["ve"].c, persist_base=32 * tt)
        xnv = apply_ln(b, Xt, XNe, bcv)
        TE.wait(ve, xnv)

        def k_out(ob, bank, mmv, tt=tt):
            bname = "p0" if ob % 2 == 0 else "p1"
            ktile = t["KF"][:, ob * CG:(ob + 1) * CG]
            SE.wait(te, mmv)
            kv_ = SE.activation(ktile, bank[:, :], AF.Identity,
                                bias=t["b_bk_c_s"][:, ob:ob + 1]).inc(se)
            b.bank_rel[bname] = (b.sems["se"], kv_)
            sq = t["SC1"][:, (ob % 2) * 512:(ob % 2) * 512 + 512]
            sqv = SE.activation(sq, ktile, AF.Square).inc(se)
            bank_wait(b, TE, "p2"); bank_wait(b, TE, "p3")
            TE.wait(se, sqv)
            TE.matmul(t["p2"][0:1, :], t["ones_col_f"][0:64, :], sq[0:64, :], start=True, stop=True)
            nv = TE.matmul(t["p3"][0:1, :], t["ones_col_f"][64:128, :], sq[64:128, :],
                           start=True, stop=True).inc(te)
            VE.wait(te, nv)
            mx0 = t["JNK"][0:1, 0:1]
            mx1 = t["JNK"][0:1, 1:2]
            VE.reduce_max(mx0, t["p2"][0:1, :], AX.X)
            VE.reduce_max(mx1, t["p3"][0:1, :], AX.X)
            h = 2 * ob
            if first_norm[0] and tt == 0:
                VE.tensor_copy(t["MQA"][0:1, h:h + 1], mx0)
                mv = VE.tensor_copy(t["MQA"][0:1, h + 1:h + 2], mx1).inc(ve)
            else:
                VE.tensor_max(t["MQA"][0:1, h:h + 1], t["MQA"][0:1, h:h + 1], mx0)
                mv = VE.tensor_max(t["MQA"][0:1, h + 1:h + 2], t["MQA"][0:1, h + 1:h + 2], mx1).inc(ve)
            b.bank_rel["p2"] = (b.sems["ve"], mv)
            b.bank_rel["p3"] = (b.sems["ve"], mv)
            SY.wait(se, kv_)
            kdst = (t["ccKa_in"][ob * 128:(ob + 1) * 128, tt * 512:(tt + 1) * 512]
                    if tt < 3 else t["ccKb_in"][ob * 128:(ob + 1) * 128, 0:512])
            sdma(b, SY, ld, kdst, ktile)

        proj_feat(b, ws2, "b_Wk", XNe, k_out, banks=("p0", "p1"))
        first_norm[0] = False
        if tt == 3:
            sync_to(b, GP, "vector")
            sdma(b, GP, gp, t["ccN_in"][0:1, 0:16], t["MQA"][0:1, 0:16])
            GP.wait(gp, gp.c)
            b.ccn_v = GP.collective_compute("AllGather", ALU.bypass,
                                          replica_groups=[[0, 1], [2, 3], [4, 5], [6, 7]],
                                          ins=[t["ccN_in"].opt()],
                                          outs=[t["ccN_out"].opt()]).inc(cc, 1)

        widxs = [ws2.load(t["b_Wv_bf"][:, q4 * 256:(q4 + 1) * 256],
                          rearr=("(c p) f -> p c f", dict(p=128)), inner=256,
                          ready=b.cast_ready["b_Wv"]) for q4 in range(4)]
        proj_tok(b, [w[1] for w in widxs], [w[0] for w in widxs], ws2, XNe,
                 t["MIX"], bias_row="b_bv_r", banks=("p2", "p3"))
        for c4 in range(4):
            SY.wait(se, se.c)
            vdst = (t["ccVa_in"][tt * 256 + c4 * 64: tt * 256 + (c4 + 1) * 64, :]
                    if tt < 3 else t["ccVb_in"][c4 * 64:(c4 + 1) * 64, :])
            sdma(b, SY, ld, vdst.rearrange("r (h f) -> (r h) f", h=2),
                 t["MIX"][:, c4 * 1024:(c4 + 1) * 1024])
        if tt == 2:
            GP.wait(ld, ld.c)
            GP.collective_compute("AllGather", ALU.bypass,
                                  replica_groups=[[0, 1], [2, 3], [4, 5], [6, 7]],
                                  ins=[t["ccKa_in"].opt()],
                                  outs=[t["ccKa_out"].opt()]).inc(cc, 1)
            b.ccab_v = GP.collective_compute("AllGather", ALU.bypass,
                                           replica_groups=[[0, 1], [2, 3], [4, 5], [6, 7]],
                                           ins=[t["ccVa_in"].opt()],
                                           outs=[t["ccVa_out"].opt()]).inc(cc, 1)
        if tt + 1 < NTT:
            ln_rows(b, lambda e, tn=tt + 1: t["X"][:, e * LC + tn * CG: e * LC + tn * CG + CG],
                    persist_base=32 * (tt + 1))

    bq_pre = [ws2.load(t["b_Wq_bf"][:, q4 * 256:(q4 + 1) * 256],
                       rearr=("(c p) f -> p c f", dict(p=128)), inner=256,
                       ready=b.cast_ready["b_Wq"]) for q4 in range(4)]
    GP.wait(ld, ld.c)
    GP.collective_compute("AllGather", ALU.bypass,
                          replica_groups=[[0, 1], [2, 3], [4, 5], [6, 7]],
                          ins=[t["ccKb_in"].opt()],
                          outs=[t["ccKb_out"].opt()]).inc(cc, 1)
    ccv = GP.collective_compute("AllGather", ALU.bypass,
                                replica_groups=[[0, 1], [2, 3], [4, 5], [6, 7]],
                                ins=[t["ccVb_in"].opt()],
                                outs=[t["ccVb_out"].opt()]).inc(cc, 1)
    b.kvb_ccv = ccv
    for sl in range(2):
        for h4 in range(4):
            VE.memset(VSsl[sl][:, h4 * 68 + 64: h4 * 68 + 65], 1.0)
    VE.drain()
    VE.tensor_copy(t["JNK"][0:1, 0:1], t["JNK"][0:1, 0:1]).inc(ve)

    for iqp in range(NTT):
        Xt = lambda e, iqp=iqp: t["X"][:, e * LC + iqp * CG: e * LC + iqp * CG + CG]
        XNe = lambda e: t["XN"][:, e * CG:(e + 1) * CG]
        QFe = lambda e: t["QF"][:, e * CG:(e + 1) * CG]
        bcv = ln_bcast(b, b.sems["ve"].c, persist_base=32 * iqp)
        xnv = apply_ln(b, Xt, XNe, bcv)
        TE.wait(ve, xnv)

        qnorm_first = [True]

        def q_out(ob, bank, mmv):
            bname = "p0" if ob % 2 == 0 else "p1"
            qtile = QFe(ob)
            SE.wait(te, mmv)
            qv_ = SE.activation(qtile, bank[:, :], AF.Identity,
                                bias=t["b_bq_c_s"][:, ob:ob + 1]).inc(se)
            b.bank_rel[bname] = (b.sems["se"], qv_)
            sq = t["SC1"][:, (ob % 2) * 512:(ob % 2) * 512 + 512]
            sqv = SE.activation(sq, qtile, AF.Square).inc(se)
            bank_wait(b, TE, "p2"); bank_wait(b, TE, "p3")
            TE.wait(se, sqv)
            TE.matmul(t["p2"][0:1, :], t["ones_col_f"][0:64, :], sq[0:64, :], start=True, stop=True)
            nv = TE.matmul(t["p3"][0:1, :], t["ones_col_f"][64:128, :], sq[64:128, :],
                           start=True, stop=True).inc(te)
            VE.wait(te, nv)
            mx0 = t["JNK"][0:1, 0:1]
            mx1 = t["JNK"][0:1, 1:2]
            VE.reduce_max(mx0, t["p2"][0:1, :], AX.X)
            VE.reduce_max(mx1, t["p3"][0:1, :], AX.X)
            h = 2 * ob
            if qnorm_first[0] or iqp == 0:
                VE.tensor_copy(t["MQB"][0:1, h:h + 1], mx0)
                mv = VE.tensor_copy(t["MQB"][0:1, h + 1:h + 2], mx1).inc(ve)
                qnorm_first[0] = False
            else:
                VE.tensor_max(t["MQB"][0:1, h:h + 1], t["MQB"][0:1, h:h + 1], mx0)
                mv = VE.tensor_max(t["MQB"][0:1, h + 1:h + 2], t["MQB"][0:1, h + 1:h + 2], mx1).inc(ve)
            b.bank_rel["p2"] = (b.sems["ve"], mv)
            b.bank_rel["p3"] = (b.sems["ve"], mv)

        proj_feat(b, ws2, "b_Wq", XNe, q_out, banks=("p0", "p1"),
                  widxs=(bq_pre if iqp == 0 else None))
        qnorm_first[0] = False
        if iqp == 0:
            GP.wait(cc, b.ccn_v)
            sdma(b, GP, gp, t["LNWS"][64:65, 0:16], t["ccN_out"][0:1, 0:16])
            sdma(b, GP, gp, t["LNWS"][64:65, 16:32], t["ccN_out"][1:2, 0:16])
            VE.wait(gp, gp.c)
            VE.tensor_max(t["MQD"][0:1, 0:16], t["LNWS"][64:65, 0:16],
                          t["LNWS"][64:65, 16:32]).inc(ve)
            SY.wait(cc, b.ccab_v)
            b.kvb_waited = False

        mrow = t["LNWS"][32:33, 0:16]
        VE.wait(ve, ve.c)
        mqv = VE.tensor_mul(mrow, t["MQB"][0:1, 0:16], t["MQD"][0:1, 0:16]).inc(ve)
        SE.wait(ve, mqv)
        sev = SE.activation(mrow, mrow, AF.Sqrt).inc(se)
        VE.wait(se, sev)
        ve_spacer(b)
        mqv = VE.tensor_single_scalar(mrow, mrow, -1.02 / 8.0, ALU.mult).inc(ve)
        bank_wait(b, TE, "p6")
        TE.wait(ve, mqv)
        bv_ = TE.matmul(t["p6"][:, 0:16], t["ones128_f"][32:33, :], mrow, start=True, stop=True).inc(te)
        VE.wait(te, bv_)
        mhv = VE.tensor_copy(t["MH"][:, :], t["p6"][:, 0:16]).inc(ve)
        b.bank_rel["p6"] = (b.sems["ve"], mhv)
        SE.wait(ve, mhv)

        KBMAX = 20 + iqp * 4
        for hs4 in range(4):
            obank = [f"p{2 + i}" for i in range(4)]
            for kb in range(KBMAX):
                par = kb % 2
                rank, kb16 = kb // 16, kb % 16
                need_mask = kb >= iqp * 4
                rel = b.flash_rel.get(par)
                if rel is not None:
                    SY.wait(rel[0], rel[1])
                kvsem = b.sems[f"kv{par}"]
                mbsem = b.sems[f"mb{par}"]
                if kb16 < 12:
                    Ksrc = t["ccKa_out"][rank * 1024 + hs4 * 256: rank * 1024 + (hs4 + 1) * 256,
                                         kb16 * 128:(kb16 + 1) * 128]
                    Vsrc = t["ccVa_out"][rank * 768 + kb16 * 64: rank * 768 + (kb16 + 1) * 64, :]
                else:
                    if not b.kvb_waited:
                        SY.wait(cc, b.kvb_ccv)
                        b.kvb_waited = True
                    kk = kb16 - 12
                    Ksrc = t["ccKb_out"][rank * 1024 + hs4 * 256: rank * 1024 + (hs4 + 1) * 256,
                                         kk * 128:(kk + 1) * 128]
                    Vsrc = t["ccVb_out"][rank * 256 + kk * 64: rank * 256 + (kk + 1) * 64, :]
                sdma(b, SY, kvsem,
                     KSsl[par].rearrange("p (c tk) -> p c tk", tk=128),
                     Ksrc.rearrange("(c p) tk -> p c tk", p=128))
                kvv = sdma(b, SY, kvsem,
                           VSsl[par].rearrange("p (h w) -> p h w", w=68)[:, :, 0:64],
                           Vsrc.rearrange("r (hh hf) -> (r hh) hf", hh=2)
                               .rearrange("p (h f) -> p h f", f=64)[:, hs4 * 4:(hs4 + 1) * 4, :])
                mbv = None
                if need_mask:
                    mrow0 = (iqp * 32 + kb) * 128
                    mbv = sdma(b, SY, mbsem, MBsl[par], t["mb"][mrow0:mrow0 + 128, :])

                pend = {}

                def emit_st(h4, kb=kb, par=par, hs4=hs4, need_mask=need_mask,
                            kvv=kvv, mbv=mbv):
                    h = hs4 * 4 + h4
                    sbname = ("p6", "p7", "p0", "p1")[h4]
                    sbank = t[sbname]
                    bank_wait(b, TE, sbname)
                    if h4 == 0:
                        TE.wait(kvsem, kvv)
                        if need_mask:
                            TE.wait(mbsem, mbv)
                    lhs = KSsl[par][64 * (h4 % 2):64 * (h4 % 2) + 64,
                                    (h4 // 2) * 128:(h4 // 2) * 128 + 128]
                    rhs = t["QF"][64 * (h % 2):64 * (h % 2) + 64, (h // 2) * CG:(h // 2) * CG + CG]
                    stv = TE.matmul(sbank[:, :], lhs, rhs, start=True, stop=True).inc(te)
                    SE.wait(te, stv)
                    pt = PTsl[h4 % 2]
                    sev = SE.activation(pt, sbank[:, :], AF.Exp, scale=0.125,
                                        bias=t["MH"][:, h:h + 1]).inc(se)
                    b.bank_rel[sbname] = (b.sems["se"], sev)
                    if need_mask:
                        VE.wait(se, sev)
                        ve_spacer(b)
                        mkv = VE.tensor_mul(pt, pt, MBsl[par]).inc(ve)
                        pend[h4] = (b.sems["ve"], mkv, pt)
                    else:
                        pend[h4] = (b.sems["se"], sev, pt)

                def emit_av(h4, kb=kb, par=par):
                    sem, v, pt = pend[h4]
                    obname = obank[h4]
                    if kb == 0:
                        bank_wait(b, TE, obname)
                    TE.wait(sem, v)
                    return TE.matmul(t[obname][0:65, :], VSsl[par][:, h4 * 68: h4 * 68 + 65],
                                     pt, start=(kb == 0), stop=(kb == KBMAX - 1)).inc(te)

                emit_st(0)
                emit_st(1)
                emit_av(0)
                emit_st(2)
                emit_av(1)
                emit_st(3)
                emit_av(2)
                last_read = emit_av(3)
                b.flash_rel[par] = (b.sems["te"], last_read)
            for h4 in range(4):
                h = hs4 * 4 + h4
                obname = obank[h4]
                obk = t[obname]
                rec = t["LNWS"][32:33, 0:512]
                VE.wait(te, b.sems["te"].c)
                rcv = VE.reciprocal(rec, obk[64:65, :]).inc(ve)
                VE.drain()
                bank_wait(b, TE, "p6")
                TE.wait(ve, rcv)
                bcv2 = TE.matmul(t["p6"][0:64, :], t["ones128_f"][32:33, 0:64],
                                 rec, start=True, stop=True).inc(te)
                VE.wait(te, bcv2)
                VE.tensor_copy(t["SC1"][0:64, 512:1024], t["p6"][0:64, :])
                dst = t["XN"][64 * (h % 2):64 * (h % 2) + 64, (h // 2) * CG:(h // 2) * CG + CG]
                nv = VE.tensor_mul(dst, obk[0:64, :], t["SC1"][0:64, 512:1024]).inc(ve)
                b.bank_rel["p6"] = (b.sems["ve"], nv)
                b.bank_rel[obname] = (b.sems["ve"], nv)

        def wo_out(ob, bank, mmv, iqp=iqp):
            bname = "p0" if ob % 2 == 0 else "p1"
            VE.wait(te, mmv)
            xsl = t["X"][:, ob * LC + iqp * CG: ob * LC + iqp * CG + CG]
            xv = VE.scalar_tensor_tensor(xsl, bank[:, :], t["b_bo_c_s"][:, ob:ob + 1],
                                         xsl, ALU.add, ALU.add).inc(ve)
            b.bank_rel[bname] = (b.sems["ve"], xv)

        VE.wait(ve, ve.c)
        proj_feat(b, ws2, "b_Wo", XNe, wo_out, banks=("p0", "p1"))
    fence(b)


def nsl(b, c):
    t = b.t
    r0, c0 = 32 * (c % 3), 128 * (c // 3)
    return (t["n_pk_s"][r0:r0 + 16, c0:c0 + 128], None)


def mixer_chunk(b, pss, cg, c4):
    t = b.t
    TE, VE, SE = b.TE, b.VE, b.SE
    te, ve, se = b.sems["te"], b.sems["ve"], b.sems["se"]
    A = t["ACT"]
    AKQ, AQK = A[:, 0:2048], A[:, 2048:4096]
    QT, KT, VT = A[:, 4096:8192], A[:, 8192:12288], A[:, 12288:16384]
    c = cg * 4 + c4
    r0, c0 = 32 * (c % 3), 128 * (c // 3)

    def qf_h(h):
        return t["QF"][64 * (h % 2):64 * (h % 2) + 64, (h // 2) * CG + c4 * 128:(h // 2) * CG + c4 * 128 + 128]

    def kf_h(h):
        return t["KF"][64 * (h % 2):64 * (h % 2) + 64, (h // 2) * CG + c4 * 128:(h // 2) * CG + c4 * 128 + 128]

    def qt_h(h):
        return QT[:, c4 * 1024 + 64 * h: c4 * 1024 + 64 * h + 64]

    def kt_h(h):
        return KT[:, c4 * 1024 + 64 * h: c4 * 1024 + 64 * h + 64]

    def vt_h(h):
        return VT[:, c4 * 1024 + 64 * h: c4 * 1024 + 64 * h + 64]

    onebf = t["ones_col_bf"]
    Q = lambda nm: t[nm][0:16, :]

    VE.tensor_copy(t["Qnw"][0:16, :], t["n_pk_s"][r0:r0 + 16, c0:c0 + 128])

    sync_to(b, b.VE, "tensor")
    VE.tensor_copy(t["RVB"][:, 0:16], t["RV"][:, :])
    VE.tensor_copy(t["RV2B"][:, 0:16], t["RV2"][:, :])
    VE.tensor_copy(t["KVB"][:, 0:512], t["KVR"][:, :])
    VE.tensor_sub(t["SC1"][:, 0:16], t["RV"][:, :], t["RVB"][:, 0:16])
    VE.tensor_sub(t["SC1"][:, 16:32], t["RV2"][:, :], t["RV2B"][:, 0:16])
    VE.tensor_sub(t["SC1"][:, 32:544], t["KVR"][:, :], t["KVB"][:, 0:512])
    VE.tensor_copy(t["RVB"][:, 16:32], t["SC1"][:, 0:16])
    VE.tensor_copy(t["RV2B"][:, 16:32], t["SC1"][:, 16:32])
    VE.tensor_copy(t["KVB"][:, 512:1024], t["SC1"][:, 32:544])
    VE.reciprocal(t["Qiw"][0:16, :], t["Qnw"][0:16, :])
    VE.drain()

    sync_to(b, b.TE, "vector")
    for h in range(H):
        bname = "p0" if h % 2 == 0 else "p1"
        bank = t[bname]
        bank_wait(b, TE, bname)
        TE.matmul(bank[:, 0:128], kf_h(h), qf_h(h), start=True, stop=True)
        mv = TE.matmul(bank[:, 128:256], qf_h(h), kf_h(h), start=False, stop=True,
                       skip_group_check=True).inc(te)
        VE.wait(te, mv)
        VE.tensor_mul(AKQ[:, h * 128:(h + 1) * 128], bank[:, 0:128], t["umask_f"][:, :])
        ev = VE.tensor_mul(AQK[:, h * 128:(h + 1) * 128], bank[:, 128:256], t["umask_f"][:, :]).inc(ve)
        b.bank_rel[bname] = (b.sems["ve"], ev)

    sync_to(b, b.TE, "vector")
    bank_wait(b, TE, "p2")
    p2 = t["p2"]
    first = True
    for h in range(H):
        hp, par = h // 2, h % 2
        ob = 64 * par
        TE.matmul(p2[:, h:h + 1], AKQ[:, h * 128:(h + 1) * 128], onebf[:, :],
                  start=first, stop=False, skip_group_check=True)
        first = False
        TE.matmul(p2[:, h:h + 1], qf_h(h), t["RVB"][ob:ob + 64, hp:hp + 1],
                  start=False, stop=False, skip_group_check=True)
        TE.matmul(p2[:, h:h + 1], qf_h(h), t["RVB"][ob:ob + 64, 16 + hp:17 + hp],
                  start=False, stop=True, skip_group_check=True)
        TE.matmul(p2[:, 32 + h:33 + h], AQK[:, h * 128:(h + 1) * 128], onebf[:, :],
                  start=False, stop=False, skip_group_check=True)
        TE.matmul(p2[:, 32 + h:33 + h], kf_h(h), t["RVB"][ob:ob + 64, 8 + hp:9 + hp],
                  start=False, stop=False, skip_group_check=True)
        TE.matmul(p2[:, 32 + h:33 + h], kf_h(h), t["RVB"][ob:ob + 64, 24 + hp:25 + hp],
                  start=False, stop=True, skip_group_check=True)
        TE.matmul(p2[:, 64 + h:65 + h], kf_h(h), onebf[ob:ob + 64, :],
                  start=False, stop=True, skip_group_check=True)
        TE.matmul(p2[:, 96 + h:97 + h], qf_h(h), onebf[ob:ob + 64, :],
                  start=False, stop=True, skip_group_check=True)
    bank_wait(b, TE, "p3")
    firstq = True
    for q, TT_ in ((0, KT), (1, QT)):
        for hp in range(8):
            lhs = TT_[:, c4 * 1024 + hp * 128: c4 * 1024 + (hp + 1) * 128]
            TE.matmul(t["p3"][:, q * 8 + hp: q * 8 + hp + 1], lhs, onebf[:, :],
                      start=firstq, stop=True, skip_group_check=True)
            firstq = False

    sync_to(b, b.VE, "tensor")
    for q in range(4):
        VE.tensor_copy(t["COLS"][:, 32 * q:32 * q + 16], p2[:, 32 * q:32 * q + 16])
    b.bank_rel["p2"] = (b.sems["ve"], bump(b, "vector"))
    sync_to(b, b.TE, "vector")
    bank_wait(b, TE, "p4")
    TE.transpose(t["p4"][0:64, 0:128], t["COLS"][:, 0:64], t["ident_f"][:, :])
    TE.transpose(t["p4"][0:64, 128:256], t["COLS"][:, 64:128], t["ident_f"][:, :])
    sync_to(b, b.VE, "tensor")
    VE.tensor_copy(Q("Qs1p"), t["p4"][0:16, 0:128])
    VE.tensor_copy(Q("Qs2p"), t["p4"][32:48, 0:128])
    VE.tensor_copy(Q("Qrk"), t["p4"][0:16, 128:256])
    VE.tensor_copy(Q("Qrq"), t["p4"][32:48, 128:256])
    b.bank_rel["p4"] = (b.sems["ve"], bump(b, "vector"))
    VE.tensor_add(t["RV"][:, :], t["RV"][:, :], t["p3"][:, 0:16])
    b.bank_rel["p3"] = (b.sems["ve"], bump(b, "vector"))

    VE.tensor_tensor_scan(Q("Qcrk"), t["ones128_f"][0:16, :], Q("Qrk"),
                          t["RSC"][0:16, 0:1], ALU.mult, ALU.add)
    VE.tensor_tensor_scan(Q("Qcrq"), t["ones128_f"][0:16, :], Q("Qrq"),
                          t["RSC"][0:16, 1:2], ALU.mult, ALU.add)
    VE.tensor_copy(t["RSC"][0:16, 0:1], t["Qcrk"][0:16, 127:128])
    VE.tensor_copy(t["RSC"][0:16, 1:2], t["Qcrq"][0:16, 127:128])
    VE.tensor_add(Q("Qw1"), Q("Qcrk"), Q("Qrq"))
    VE.tensor_add(Q("Qsc"), Q("Qcrq"), Q("Qrk"))
    VE.scalar_tensor_tensor(Q("Qw2"), Q("Qw1"), EPS, Q("Qs1p"), ALU.mult, ALU.add)
    VE.scalar_tensor_tensor(Q("Qsasc"), Q("Qsc"), EPS, Q("Qs2p"), ALU.mult, ALU.add)
    VE.tensor_single_scalar(Q("Qw1"), Q("Qw2"), EPS2_64, ALU.add)
    VE.tensor_single_scalar(Q("Qsc"), Q("Qsasc"), EPS2_64, ALU.add)
    VE.reciprocal(Q("Qinv1"), Q("Qw1"))
    VE.drain()
    VE.reciprocal(Q("Qw2"), Q("Qsc"))
    VE.drain()
    VE.tensor_mul(Q("Qsi"), Q("Qinv1"), Q("Qnw"))
    VE.tensor_mul(Q("Qso"), Q("Qw2"), Q("Qnw"))

    sync_to(b, b.TE, "vector")
    bank_wait(b, TE, "p4")
    TE.transpose(t["p4"][:, 0:16], Q("Qsi"), t["ident_f"][0:16, 0:16])
    TE.transpose(t["p4"][:, 32:48], Q("Qso"), t["ident_f"][0:16, 0:16])
    sync_to(b, b.VE, "tensor")
    VE.tensor_copy(t["COLS2"][:, 0:16], t["p4"][:, 0:16])
    VE.tensor_copy(t["COLS2"][:, 32:48], t["p4"][:, 32:48])
    b.bank_rel["p4"] = (b.sems["ve"], bump(b, "vector"))
    VE.tensor_copy(t["COLS2B"][:, 0:64], t["COLS2"][:, :])
    VE.drain()
    VE.tensor_sub(t["SC1"][:, 0:64], t["COLS2"][:, :], t["COLS2B"][:, 0:64])
    VE.drain()
    VE.tensor_copy(t["COLS2B"][:, 64:128], t["SC1"][:, 0:64])
    VE.drain()

    sync_to(b, b.TE, "vector")
    bank_wait(b, TE, "p2")
    first = True
    for h in range(H):
        hp, par = h // 2, h % 2
        ob = 64 * par
        TE.matmul(p2[:, h:h + 1], AKQ[:, h * 128:(h + 1) * 128], t["COLS2B"][:, 32 + h:33 + h],
                  start=first, stop=False, skip_group_check=True)
        first = False
        TE.matmul(p2[:, h:h + 1], AKQ[:, h * 128:(h + 1) * 128], t["COLS2B"][:, 96 + h:97 + h],
                  start=False, stop=False, skip_group_check=True)
        TE.matmul(p2[:, h:h + 1], qf_h(h), t["RV2B"][ob:ob + 64, hp:hp + 1],
                  start=False, stop=False, skip_group_check=True)
        TE.matmul(p2[:, h:h + 1], qf_h(h), t["RV2B"][ob:ob + 64, 16 + hp:17 + hp],
                  start=False, stop=True, skip_group_check=True)
        TE.matmul(p2[:, 32 + h:33 + h], AQK[:, h * 128:(h + 1) * 128], t["COLS2B"][:, h:h + 1],
                  start=False, stop=False, skip_group_check=True)
        TE.matmul(p2[:, 32 + h:33 + h], AQK[:, h * 128:(h + 1) * 128], t["COLS2B"][:, 64 + h:65 + h],
                  start=False, stop=False, skip_group_check=True)
        TE.matmul(p2[:, 32 + h:33 + h], kf_h(h), t["RV2B"][ob:ob + 64, 8 + hp:9 + hp],
                  start=False, stop=False, skip_group_check=True)
        TE.matmul(p2[:, 32 + h:33 + h], kf_h(h), t["RV2B"][ob:ob + 64, 24 + hp:25 + hp],
                  start=False, stop=True, skip_group_check=True)
    bank_wait(b, TE, "p3")
    firstq = True
    for q in range(2):
        for h in range(H):
            hp, par = h // 2, h % 2
            col = 64 + q * 16 + par * 8 + hp
            lhs = kt_h(h) if q == 0 else qt_h(h)
            hicol = t["COLS2B"][:, (32 + h):(33 + h)] if q == 0 else t["COLS2B"][:, h:h + 1]
            locol = t["COLS2B"][:, (96 + h):(97 + h)] if q == 0 else t["COLS2B"][:, 64 + h:65 + h]
            TE.matmul(t["p3"][0:64, col:col + 1], lhs, hicol, start=firstq, stop=False,
                      skip_group_check=True)
            firstq = False
            TE.matmul(t["p3"][0:64, col:col + 1], lhs, locol, start=False, stop=True,
                      skip_group_check=True)

    sync_to(b, b.VE, "tensor")
    VE.tensor_copy(t["COLS"][:, 0:16], p2[:, 0:16])
    VE.tensor_copy(t["COLS"][:, 32:48], p2[:, 32:48])
    b.bank_rel["p2"] = (b.sems["ve"], bump(b, "vector"))
    sync_to(b, b.TE, "vector")
    bank_wait(b, TE, "p4")
    TE.transpose(t["p4"][0:64, 0:128], t["COLS"][:, 0:64], t["ident_f"][:, :])
    sync_to(b, b.VE, "tensor")
    VE.tensor_copy(Q("Qs3p"), t["p4"][0:16, 0:128])
    VE.tensor_copy(Q("Qs4p"), t["p4"][32:48, 0:128])
    b.bank_rel["p4"] = (b.sems["ve"], bump(b, "vector"))
    for q in range(2):
        VE.tensor_add(t["RV2"][0:64, q * 8:(q + 1) * 8], t["RV2"][0:64, q * 8:(q + 1) * 8],
                      t["p3"][0:64, 64 + q * 16: 64 + q * 16 + 8])
        VE.tensor_add(t["RV2"][64:128, q * 8:(q + 1) * 8], t["RV2"][64:128, q * 8:(q + 1) * 8],
                      t["p3"][0:64, 64 + q * 16 + 8: 64 + q * 16 + 16])
    b.bank_rel["p3"] = (b.sems["ve"], bump(b, "vector"))

    VE.tensor_mul(Q("Qw1"), Q("Qrk"), Q("Qso"))
    VE.tensor_mul(Q("Qw2"), Q("Qrq"), Q("Qsi"))
    VE.tensor_tensor_scan(Q("Qcrk"), t["ones128_f"][0:16, :], Q("Qw1"),
                          t["RSC"][0:16, 2:3], ALU.mult, ALU.add)
    VE.tensor_tensor_scan(Q("Qcrq"), t["ones128_f"][0:16, :], Q("Qw2"),
                          t["RSC"][0:16, 3:4], ALU.mult, ALU.add)
    VE.tensor_copy(t["RSC"][0:16, 2:3], t["Qcrk"][0:16, 127:128])
    VE.tensor_copy(t["RSC"][0:16, 3:4], t["Qcrq"][0:16, 127:128])
    VE.tensor_add(Q("Qw1"), Q("Qcrk"), Q("Qrq"))
    VE.tensor_add(Q("Qw2"), Q("Qcrq"), Q("Qrk"))
    VE.scalar_tensor_tensor(Q("Qsc"), Q("Qw1"), EPS, Q("Qs3p"), ALU.mult, ALU.add)
    VE.scalar_tensor_tensor(Q("Qsasc"), Q("Qw2"), EPS, Q("Qs4p"), ALU.mult, ALU.add)
    VE.tensor_single_scalar(Q("Qw1"), Q("Qsc"), EPS2_64, ALU.add)
    VE.tensor_single_scalar(Q("Qw2"), Q("Qsasc"), EPS2_64, ALU.add)
    cs3 = VE.tensor_mul(Q("Qsc"), Q("Qw1"), Q("Qiw")).inc(ve)
    cl0 = VE.tensor_mul(Q("Qsasc"), Q("Qw2"), Q("Qiw")).inc(ve)
    SE.wait(ve, cs3)
    sav = SE.activation(Q("Qsa"), Q("Qsc"), AF.Sigmoid).inc(se)
    VE.tensor_scalar(Q("Qw2"), Q("Qsasc"), 1.0, -1.0, ALU.min, ALU.max)
    cl = bump(b, "vector")
    SE.wait(ve, cl)
    ev_ = SE.activation(Q("Qe"), Q("Qw2"), AF.Exp).inc(se)
    VE.wait(se, ev_)
    ve_spacer(b)
    VE.tensor_tensor_scan(Q("Qccs"), t["ones128_f"][0:16, :], Q("Qe"),
                          t["RSC"][0:16, 4:5], ALU.mult, ALU.add)
    VE.drain()
    VE.reciprocal(Q("Qw1"), Q("Qccs"))
    VE.drain()
    VE.tensor_copy(t["RSC"][0:16, 4:5], t["Qccs"][0:16, 127:128])
    VE.tensor_mul(Q("Qw2"), Q("Qe"), Q("Qw1"))
    VE.wait(se, sav)
    ve_spacer(b)
    VE.tensor_mul(Q("Qsasc"), Q("Qsa"), Q("Qinv1"))
    VE.tensor_mul(Q("Qsc"), Q("Qw2"), Q("Qnw"))

    sync_to(b, b.TE, "vector")
    bank_wait(b, TE, "p4")
    TE.transpose(t["p4"][:, 0:16], Q("Qsc"), t["ident_f"][0:16, 0:16])
    TE.transpose(t["p4"][:, 32:48], Q("Qsasc"), t["ident_f"][0:16, 0:16])
    sync_to(b, b.VE, "tensor")
    VE.tensor_copy(t["COLS2"][:, 0:16], t["p4"][:, 0:16])
    VE.tensor_copy(t["COLS2"][:, 32:48], t["p4"][:, 32:48])
    b.bank_rel["p4"] = (b.sems["ve"], bump(b, "vector"))

    kv_first = True
    for h in range(H):
        hp, par = h // 2, h % 2
        sync_to(b, b.VE, "tensor")
        vsv = VE.tensor_scalar_mul(t["VSC"][:, :], vt_h(h), t["COLS2"][:, h:h + 1]).inc(ve)
        pb = "p5" if par == 0 else "p6"
        if kv_first:
            bank_wait(b, TE, "p5")
            bank_wait(b, TE, "p6")
        TE.wait(ve, vsv)
        TE.matmul(t[pb][0:64, hp * 64:(hp + 1) * 64], kt_h(h), t["VSC"][:, :],
                  start=kv_first or (h == 1), stop=True, skip_group_check=True)
        kv_first = False
        if pss == 1:
            bank_wait(b, TE, "p7")
            TE.matmul(t["p7"][:, 0:64], AKQ[:, h * 128:(h + 1) * 128], t["VSC"][:, :],
                      start=True, stop=False)
            TE.matmul(t["p7"][:, 0:64], qf_h(h), t["KVB"][64 * par:64 * par + 64, hp * 64:(hp + 1) * 64],
                      start=False, stop=False)
            ov = TE.matmul(t["p7"][:, 0:64], qf_h(h),
                           t["KVB"][64 * par:64 * par + 64, 512 + hp * 64: 512 + (hp + 1) * 64],
                           start=False, stop=True).inc(te)
            VE.wait(te, ov)
            mxv = VE.tensor_scalar_mul(t["SC1"][:, 0:64], t["p7"][:, 0:64],
                                       t["COLS2"][:, 32 + h:33 + h]).inc(ve)
            b.bank_rel["p7"] = (b.sems["ve"], mxv)
            sync_to(b, b.TE, "vector")
            bank_wait(b, TE, "p4")
            TE.transpose(t["p4"][0:64, 0:128], t["SC1"][:, 0:64], t["ident_f"][:, :])
            sync_to(b, b.VE, "tensor")
            cpv = VE.tensor_copy(t["MIX"][64 * par:64 * par + 64, hp * CG + c4 * 128: hp * CG + (c4 + 1) * 128],
                                 t["p4"][0:64, 0:128]).inc(ve)
            b.bank_rel["p4"] = (b.sems["ve"], cpv)
    sync_to(b, b.VE, "tensor")
    VE.tensor_add(t["KVR"][0:64, :], t["KVR"][0:64, :], t["p5"][0:64, :])
    VE.tensor_add(t["KVR"][64:128, :], t["KVR"][64:128, :], t["p6"][0:64, :])
    v = bump(b, "vector")
    b.bank_rel["p5"] = (b.sems["ve"], v)
    b.bank_rel["p6"] = (b.sems["ve"], v)


def emit_attn_a(b):
    t = b.t
    TE, VE, SE, GP, SY = b.TE, b.VE, b.SE, b.GP, b.SY
    te, ve, se, gp, ld, cc = (b.sems[k] for k in ("te", "ve", "se", "gp", "ld", "cc"))
    A = t["ACT"]
    QT, KT, VT = A[:, 4096:8192], A[:, 8192:12288], A[:, 12288:16384]
    ws2 = WStream(b, "WS2", 2048, 4, semgroup="w2s")

    for tt in range(NTT):
        Xt = lambda e, tt=tt: t["X"][:, e * LC + tt * CG: e * LC + tt * CG + CG]
        ln_rows(b, Xt, persist_base=32 * tt)
    fence(b)

    import os as _os_a
    trunc = int(_os_a.environ.get("ATTNA_TRUNC", "99"))
    if trunc <= 1:
        return
    for pss in range(2):
        if pss == 1:
            if trunc <= 4:
                return
            sdma(b, SY, ld, t["cc_a_in"][:, 0:16], t["RV"][:, :])
            sdma(b, SY, ld, t["cc_a_in"][:, 16:32], t["RV2"][:, :])
            sdma(b, SY, ld, t["cc_a_in"][:, 32:544], t["KVR"][:, :])
            sdma(b, SY, ld, t["cc_a_in"][0:16, 544:552], t["RSC"][:, :])
            GP.wait(ld, ld.c)
            ccv = GP.collective_compute("AllGather", ALU.bypass,
                                        replica_groups=[[0, 1], [2, 3], [4, 5], [6, 7]],
                                        ins=[t["cc_a_in"].opt()],
                                        outs=[t["cc_a_out"].opt()]).inc(cc, 1)
            b.a_ccv = ccv
            if trunc <= 5:
                return

        for cg in range(NTT):
            if pss == 0 and cg >= 1 and trunc <= 3:
                break
            if pss == 1 and cg >= trunc - 5 and trunc <= 8:
                return
            if pss == 1 and trunc == 9 and cg in (1, 2):
                continue
            Xt = lambda e, cg=cg: t["X"][:, e * LC + cg * CG: e * LC + cg * CG + CG]
            XNe = lambda e: t["XN"][:, e * CG:(e + 1) * CG]
            bcv = ln_bcast(b, b.sems["ve"].c, persist_base=32 * cg)
            xnv = apply_ln(b, Xt, XNe, bcv)
            TE.wait(ve, xnv)
            direct = (pss == 0 and cg == 0)
            for wname, feat_dst, tok_dst, bias_c, bias_r, sig in (
                    ("a_Wq", "QF", QT, "a_bq_c_s", "a_bq_r", True),
                    ("a_Wk", "KF", KT, "a_bk_c_s", "a_bk_r", True),
                    ("a_Wv", None, VT, "a_bv_c_s", "a_bv_r", False)):
                wsrc = t[wname] if direct else t[wname + "_bf"]
                wready = None if direct else b.cast_ready[wname]
                widxs = [ws2.load(wsrc[:, q4 * 256:(q4 + 1) * 256],
                                  rearr=("(c p) f -> p c f", dict(p=128)), inner=256,
                                  ready=wready) for q4 in range(4)]
                if feat_dst is not None:
                    fd = t[feat_dst]
                    proj_feat(b, ws2, wname, XNe,
                              (lambda ob, fd=fd: fd[:, ob * CG:(ob + 1) * CG],),
                              bias_col=bias_c, sigmoid=sig, banks=("p0", "p1"),
                              widxs=widxs, release=False)
                proj_tok(b, [w[1] for w in widxs], [w[0] for w in widxs], ws2, XNe,
                         tok_dst, bias_row=bias_r, sigmoid=sig, banks=("p0", "p1"))
            fence(b)
            if pss == 1 and cg == 0:
                GP.wait(cc, b.a_ccv)
                sdma(b, GP, gp, t["RV"][:, :], t["cc_a_out"][0:128, 0:16])
                sdma(b, GP, gp, t["RV2"][:, :], t["cc_a_out"][0:128, 16:32])
                sdma(b, GP, gp, t["KVR"][:, :], t["cc_a_out"][0:128, 32:544])
                sdma(b, GP, gp, t["RSC"][:, :], t["cc_a_out"][0:16, 544:552])
                VE.wait(gp, gp.c)
                VE.tensor_scalar_mul(t["RV"][:, :], t["RV"][:, :], t["role_mask_s"][:, 0:1])
                VE.tensor_scalar_mul(t["RV2"][:, :], t["RV2"][:, :], t["role_mask_s"][:, 0:1])
                VE.tensor_scalar_mul(t["KVR"][:, :], t["KVR"][:, :], t["role_mask_s"][:, 0:1])
                VE.tensor_scalar_mul(t["RSC"][:, :], t["RSC"][:, :], t["role_mask_s"][0:16, 0:1])
            if pss == 0 and cg == 0:
                cast_phase(b, 0)
            if pss == 0 and cg < 3:
                cast_phase(b, cg + 1)
            if trunc <= 2:
                return
            for c4 in range(4):
                mixer_chunk(b, pss, cg, c4)
            if pss == 1:
                attn_a_tail(b, cg, ws2)
            fence(b)


def attn_a_tail(b, cg, ws2):
    t = b.t
    TE, VE, SE = b.TE, b.VE, b.SE
    te, ve, se = b.sems["te"], b.sems["ve"], b.sems["se"]
    Xt = lambda e: t["X"][:, e * LC + cg * CG: e * LC + cg * CG + CG]
    XNe = lambda e: t["XN"][:, e * CG:(e + 1) * CG]
    MIXe = lambda e: t["MIX"][:, e * CG:(e + 1) * CG]
    SC0e = lambda e: t["SC0"][:, e * CG:(e + 1) * CG]

    def wo_out(ob, bank, mmv):
        bname = "p0" if ob % 2 == 0 else "p1"
        s1sl = t["SC1"][:, (ob % 2) * 512:(ob % 2) * 512 + 512]
        VE.wait(te, mmv)
        VE.tensor_mul(s1sl, Xt(ob), t["BC"][:, 0:512])
        VE.tensor_add(s1sl, s1sl, t["BC"][:, 512:1024])
        xv = VE.scalar_tensor_tensor(SC0e(ob), bank[:, :], t["a_bo_c_s"][:, ob:ob + 1],
                                     s1sl, ALU.add, ALU.add).inc(ve)
        b.bank_rel[bname] = (b.sems["ve"], xv)

    proj_feat(b, ws2, "a_Wo", MIXe, wo_out, banks=("p0", "p1"))
    fence(b)
    rv = ln_rows(b, SC0e)
    bcv = ln_bcast(b, rv)
    hv = apply_ln(b, SC0e, SC0e, bcv, gcol="a_ln1_g_c_s", bcol="a_ln1_b_c_s")
    VE.wait(ve, hv)
    for e in range(8):
        hbv = VE.tensor_copy(XNe(e), SC0e(e)).inc(ve)
    act_v = None
    for hb2 in range(16):
        idx, wdst = ws2.load(t["a_W1_bf"][:, hb2 * 256:(hb2 + 1) * 256],
                             rearr=("(c p) h -> p c h", dict(p=128)), inner=256,
                             ready=b.cast_ready["a_W1"])
        for sub in range(2):
            hb = hb2 * 2 + sub
            bname = ("p0", "p1", "p2", "p3")[hb % 4]
            bank = t[bname]
            bank_wait(b, TE, bname)
            if sub == 0:
                ws2.wait_on(TE, idx)
            if hb == 0:
                TE.wait(ve, hbv)
            for e in range(8):
                op = TE.matmul(bank[:, :],
                               wdst[:, e * 256 + sub * 128: e * 256 + sub * 128 + 128],
                               XNe(e), start=(e == 0), stop=(e == 7))
            mmv = op.inc(b.sems["te"])
            if sub == 1:
                ws2.release(idx, b.sems["te"], mmv)
            SE.wait(te, mmv)
            av = SE.activation(t["ACT"][:, hb * 512:(hb + 1) * 512], bank[:, :], AF.Gelu,
                               bias=t["a_b1_c_s"][:, hb:hb + 1]).inc(b.sems["se"])
            b.bank_rel[bname] = (b.sems["se"], av)
            act_v = av
    for ob in range(8):
        bname = "p2" if ob % 2 == 0 else "p3"
        bank = t[bname]
        for half in range(2):
            idx, wdst = ws2.load(
                t["a_W2_bf"][half * 2048:(half + 1) * 2048, ob * 128:(ob + 1) * 128],
                rearr=("(c p) d -> p c d", dict(p=128)), inner=128,
                ready=b.cast_ready["a_W2"])
            if half == 0:
                bank_wait(b, TE, bname)
                TE.wait(se, act_v)
            ws2.wait_on(TE, idx)
            for c16 in range(16):
                hb = half * 16 + c16
                op = TE.matmul(bank[:, :], wdst[:, c16 * 128:(c16 + 1) * 128],
                               t["ACT"][:, hb * 512:(hb + 1) * 512],
                               start=(hb == 0), stop=(hb == 31))
            mmv = op.inc(b.sems["te"])
            ws2.release(idx, b.sems["te"], mmv)
        VE.wait(te, mmv)
        uv = VE.scalar_tensor_tensor(SC0e(ob), bank[:, :], t["a_b2_c_s"][:, ob:ob + 1],
                                     SC0e(ob), ALU.add, ALU.add).inc(b.sems["ve"])
        b.bank_rel[bname] = (b.sems["ve"], uv)
    fence(b)
    rv = ln_rows(b, SC0e)
    bcv = ln_bcast(b, rv)
    VE.wait(ve, bcv)
    for e in range(8):
        s1sl = t["SC1"][:, (e % 2) * 512:(e % 2) * 512 + 512]
        VE.tensor_mul(s1sl, SC0e(e), t["BC"][:, 0:512])
        VE.tensor_add(s1sl, s1sl, t["BC"][:, 512:1024])
        VE.tensor_scalar(s1sl, s1sl, t["a_ln2_g_c_s"][:, e:e + 1], t["a_ln2_b_c_s"][:, e:e + 1],
                         ALU.mult, ALU.add)
        VE.tensor_add(Xt(e), Xt(e), s1sl)


def emit_lntest(b):
    t = b.t
    for tt in range(NTT):
        Xt = lambda e, tt=tt: t["X"][:, e * LC + tt * CG: e * LC + tt * CG + CG]
        rv = ln_rows(b, Xt)
        bcv = ln_bcast(b, rv)
        apply_ln(b, Xt, Xt, bcv)
        fence(b)


def emit_qctest(b):
    t = b.t
    b.VE.tensor_copy(t["X"][:, 0:32], t["f_c1"][:, :])
    b.VE.tensor_copy(t["X"][:, 32:64], t["f_c2"][:, :])
    b.VE.tensor_copy(t["X"][:, 64:96], t["COLS"][:, 0:32])
    b.VE.tensor_copy(t["X"][:, 96:128], t["COLS"][:, 32:64])
    b.VE.tensor_copy(t["X"][:, 128:160], t["COLS"][:, 64:96])
    b.VE.tensor_copy(t["X"][:, 160:192], t["COLS"][:, 96:128])
    b.VE.tensor_copy(t["X"][:, 192:224], t["COLS2"][:, 0:32])
    fence(b)


def epilogue_out(b):
    t = b.t
    fence(b)
    ld = b.sems["ld"]
    for e in range(8):
        sdma(b, b.SY, ld, t["xout"][e * 128:(e + 1) * 128, :], t["X"][:, e * LC:(e + 1) * LC])
    b.GP.wait(ld, ld.c)


def dump_dbg(b, i):
    t = b.t
    fence(b)
    ld = b.sems["ld"]
    for e in range(8):
        sdma(b, b.SY, ld, t[f"dbg{i}"][e * 128:(e + 1) * 128, :], t["X"][:, e * LC:(e + 1) * LC])
    fence(b)


_PHASES_ALL = ("attn_a", "ffn1", "attn_b", "ffn2")


def build_program(phases=_PHASES_ALL, dbg=0, dbg_rows=False):
    nc = bass.Bass("TRN2", target_bir_lowering=False, debug=False)
    b = B(nc)
    b.dbg_rows = dbg_rows
    import os as _os
    b.dbg_fence = int(_os.environ.get('DBGFENCE', '0'))
    declare_io(nc, b.t, dbg=dbg)
    with ExitStack() as ctx:
        alloc_sbuf(b, ctx)
        for name in ("ld", "gp", "te", "ve", "se", "cc",
                     "w1s0", "w1s1", "w2s0", "w2s1", "w2s2", "w2s3",
                     "kv0", "kv1", "mb0", "mb1", "cw0", "cw1", "cw2", "cw3"):
            b.sem(ctx, name)
        b.bank_rel = {}
        b.flash_rel = {}
        prologue(b)
        quantum_consts(b)
        ndbg = 0
        for ph in phases:
            if ph == "attn_a":
                emit_attn_a(b)
            elif ph in ("ffn1", "ffn2"):
                emit_ffn(b, "f")
            elif ph == "attn_b":
                emit_attn_b(b)
            elif ph == "dbg":
                dump_dbg(b, ndbg)
                ndbg += 1
            elif ph == "lntest":
                emit_lntest(b)
            elif ph == "qctest":
                emit_qctest(b)
            else:
                raise ValueError(ph)
        epilogue_out(b)
        b.replay()
    return nc


import os as _os_k


def kernel(**inputs):
    ph = _os_k.environ.get("KPHASES", "")
    nc = build_program(phases=tuple(ph.split(",")) if ph else _PHASES_ALL)
    in_maps = [make_host_inputs(inputs, c) for c in range(8)]
    res = run_bass_kernel_spmd(nc, in_maps, core_ids=list(range(8)))
    out = np.zeros((4, 4096, 1024), np.float32)
    for c in range(8):
        b_, r = c // 2, c % 2
        out[b_, r * LC:(r + 1) * LC, :] = np.asarray(res.results[c]["xout"]).T
    return out


# version 17
# speedup vs baseline: 1.1933x; 1.1933x over previous
import numpy as np
import ml_dtypes
from contextlib import ExitStack

import concourse.bass as bass
import concourse.mybir as mybir
from concourse.bass_utils import run_bass_kernel_spmd

F32 = mybir.dt.float32
BF16 = mybir.dt.bfloat16
AF = mybir.ActivationFunctionType
ALU = mybir.AluOpType
AX = mybir.AxisListType

D = 1024
FF = 4096
H = 16
LC = 2048
NCH = 16
CG = 512
NTT = LC // CG
EPS = 1e-6
EPS2_64 = 64.0 * EPS * EPS
LNEPS = 1e-5
TWO_PI = float(2.0 * np.pi)
HALF_PI = float(np.pi / 2.0)

ENGS = ("sync", "gpsimd", "tensor", "vector", "scalar")

C8L = ("a_bq_c", "a_bk_c", "a_bv_c", "a_bo_c", "a_b2_c",
       "a_ln1_g_c", "a_ln1_b_c", "a_ln2_g_c", "a_ln2_b_c",
       "f_b2_c", "b_bq_c", "b_bk_c", "b_bv_c", "b_bo_c")
C32L = ("a_b1_c", "f_b1_c", "f_qp_c", "f_qwr_c", "f_qwi_c")


class Sem:
    __slots__ = ("h", "c", "name")

    def __init__(self, h, name):
        self.h, self.c, self.name = h, 0, name


class Op:
    __slots__ = ("fn", "incs", "is_wait", "src", "kind")

    def __init__(self, fn, is_wait=False, kind=None):
        self.fn = fn
        self.incs = []
        self.is_wait = is_wait
        self.src = None
        self.kind = kind

    def inc(self, sem, n=1):
        self.incs.append((sem, n))
        sem.c += n
        return sem.c


def _ap_span(x):
    try:
        nm = x.tensor.name
        lo = x.offset
        hi = lo + 1
        for st, ct in x.ap:
            hi += abs(st) * (ct - 1)
        return (nm, lo, hi)
    except Exception:
        return None


def _overlap(span, lst):
    nm, lo, hi = span
    for (n2, l2, h2) in lst:
        if n2 == nm and lo < h2 and l2 < hi:
            return True
    return False


class ES:

    def __init__(self, prog, key):
        self._prog = prog
        self._key = key
        self._haz = key in ("vector", "scalar")
        self._w = []
        self._r = []

    def _auto_drain(self, name, a, k):
        if not self._haz:
            return
        if name == "drain":
            self._w = []
            self._r = []
            return
        aps = [s for s in (list(a) + list(k.values())) if s is not None
               for s in [_ap_span(s)] if s is not None]
        if not aps:
            return
        outs = aps[:1]
        if "accum_out" in k and k["accum_out"] is not None:
            sp = _ap_span(k["accum_out"])
            if sp:
                outs.append(sp)
        ins = aps[1:]
        conflict = any(_overlap(s, self._w) for s in ins) or                    any(_overlap(s, self._w) or _overlap(s, self._r) for s in outs)
        if conflict:
            def dfn(eng):
                return eng.drain()
            self._prog[self._key].append(Op(dfn))
            self._w = []
            self._r = []
        self._w.extend(outs)
        self._r.extend(ins)

    def __getattr__(self, name):
        if name.startswith("_"):
            raise AttributeError(name)
        def rec(*a, **k):
            self._auto_drain(name, a, k)
            def fn(eng, _n=name, _a=a, _k=k):
                return getattr(eng, _n)(*_a, **_k)
            op = Op(fn, kind=name)
            import traceback as _tb
            fr = _tb.extract_stack(limit=2)[0]
            op.src = f"{name} @ {fr.name}:{fr.lineno}"
            self._prog[self._key].append(op)
            return op
        return rec

    def wait(self, sem, v):
        assert 0 <= v <= sem.c, f"wait {sem.name}>={v} but only {sem.c} emitted"
        if v == 0:
            return
        def fn(eng):
            return eng.wait_ge(sem.h, v)
        self._prog[self._key].append(Op(fn, is_wait=True))


class B:
    def __init__(self, nc):
        self.nc = nc
        self.prog = {k: [] for k in ENGS}
        self.SY = ES(self.prog, "sync")
        self.GP = ES(self.prog, "gpsimd")
        self.TE = ES(self.prog, "tensor")
        self.VE = ES(self.prog, "vector")
        self.SE = ES(self.prog, "scalar")
        self.sems = {}
        self.t = {}

    def sem(self, ctx, name):
        s = Sem(ctx.enter_context(self.nc.semaphore(name)), name)
        self.sems[name] = s
        return s

    def replay(self):
        with self.nc.Block() as blk:
            def runner(key):
                def _(eng):
                    for op in self.prog[key]:
                        try:
                            ins = op.fn(eng)
                        except Exception:
                            print(f"REPLAY FAIL on {key} op: {getattr(op, 'src', '?')}")
                            raise
                        for (sem, n) in op.incs:
                            ins.then_inc(sem.h, n)
                return _
            blk.sync(runner("sync"))
            blk.gpsimd(runner("gpsimd"))
            blk.tensor(runner("tensor"))
            blk.vector(runner("vector"))
            blk.scalar(runner("scalar"))


def bump(b, key):
    semname = {"tensor": "te", "vector": "ve", "scalar": "se"}[key]
    sem = b.sems[semname]
    ops = b.prog[key]
    for op in reversed(ops):
        if not op.is_wait and op.kind != "dma_start":
            if not any(s is sem for s, _ in op.incs):
                op.inc(sem)
            return sem.c
    return sem.c


def sync_to(b, es, *producers):
    for p in producers:
        if p in ("tensor", "vector", "scalar"):
            v = bump(b, p)
            es.wait(b.sems[{"tensor": "te", "vector": "ve", "scalar": "se"}[p]], v)
        else:
            es.wait(b.sems[p], b.sems[p].c)


def fence(b):
    for k in ("tensor", "vector", "scalar"):
        bump(b, k)
    snap = [(k, b.sems[k].c) for k in ("te", "ve", "se", "gp", "ld")]
    for es, own in ((b.TE, "te"), (b.VE, "ve"), (b.SE, "se"), (b.GP, "gp"), (b.SY, "ld")):
        for k, v in snap:
            if k != own and v > 0:
                es.wait(b.sems[k], v)



def _col_form(v):
    v = np.asarray(v, np.float32)
    n = v.shape[0] // 128
    return np.ascontiguousarray(v.reshape(n, 128).T)


def _nb_pack(vals):
    out = np.zeros((128, 768), np.float32)
    for c in range(NCH):
        sl = vals[c * 128:(c + 1) * 128]
        out[32 * (c % 3):32 * (c % 3) + 16, 128 * (c // 3):128 * (c // 3) + 128] = sl[None, :]
    return out


def make_host_inputs(inputs, core):
    b_, r = core // 2, core % 2
    d = {}
    x = np.asarray(inputs["L0_attn_a_inputs"], np.float32)
    d["xin"] = np.ascontiguousarray(x[b_, r * LC:(r + 1) * LC, :].T)

    for nm in ("Wq", "Wk", "Wv", "Wo"):
        d[f"a_{nm}"] = np.ascontiguousarray(np.asarray(inputs[f"L0_attn_a_{nm}"], np.float32))
    for nm in ("bq", "bk", "bv", "bo"):
        d[f"a_{nm}_c"] = _col_form(inputs[f"L0_attn_a_{nm}"])
    for nm in ("bq", "bk", "bv"):
        d[f"a_{nm}_r"] = np.asarray(inputs[f"L0_attn_a_{nm}"], np.float32).reshape(1, D).copy()
    d["a_W1"] = np.ascontiguousarray(np.asarray(inputs["L0_attn_a_W1"], np.float32))
    d["a_W2"] = np.ascontiguousarray(np.asarray(inputs["L0_attn_a_W2"], np.float32))
    d["a_b1_c"] = _col_form(inputs["L0_attn_a_b1"])
    d["a_b2_c"] = _col_form(inputs["L0_attn_a_b2"])
    for nm in ("ln1_g", "ln1_b", "ln2_g", "ln2_b"):
        d[f"a_{nm}_c"] = _col_form(inputs[f"L0_attn_a_{nm}"])

    d["f_W1"] = np.ascontiguousarray(np.asarray(inputs["L0_ffn_W1"], np.float32))
    d["f_W2"] = np.ascontiguousarray(np.asarray(inputs["L0_ffn_W2"], np.float32))
    d["f_b1_c"] = _col_form(inputs["L0_ffn_b1"])
    d["f_b2_c"] = _col_form(inputs["L0_ffn_b2"])
    d["f_qp_c"] = _col_form(inputs["L0_ffn_quantum_phases"])
    d["f_qwr_c"] = _col_form(inputs["L0_ffn_quantum_weights_real"])
    d["f_qwi_c"] = _col_form(inputs["L0_ffn_quantum_weights_imag"])

    for nm in ("Wq", "Wk", "Wv"):
        w = np.asarray(inputs[f"L0_attn_b_{nm}"], np.float32)
        d[f"b_{nm}"] = np.ascontiguousarray(w.transpose(1, 0, 2).reshape(D, D))
    for nm in ("bq", "bk", "bv"):
        v = np.asarray(inputs[f"L0_attn_b_{nm}"], np.float32).reshape(-1)
        d[f"b_{nm}_c"] = _col_form(v)
    d["b_bv_r"] = np.asarray(inputs["L0_attn_b_bv"], np.float32).reshape(1, D).copy()
    d["b_Wo"] = np.ascontiguousarray(np.asarray(inputs["L0_attn_b_Wo"], np.float32))
    d["b_bo_c"] = _col_form(inputs["L0_attn_b_bo"])

    pos = np.arange(LC, dtype=np.float32) + 1.0 + r * LC
    d["n_pk"] = _nb_pack(pos)
    d["role_mask"] = np.full((128, 1), float(r), np.float32)

    qpos = np.arange(LC, dtype=np.int64) + r * LC
    kpos = np.arange(4096, dtype=np.int64)
    mb = np.zeros((4, 32, 128, 512), np.float32)
    for iqp in range(4):
        qp = qpos[iqp * 512:(iqp + 1) * 512]
        for kb in range(32):
            kp = kpos[kb * 128:(kb + 1) * 128]
            mb[iqp, kb] = (kp[:, None] <= qp[None, :]).astype(np.float32)
    d["mb"] = mb.reshape(4 * 32 * 128, 512).astype(ml_dtypes.bfloat16)
    d["cpk"] = np.concatenate([d.pop(nm) for nm in C8L + C32L], axis=1)
    return d



def declare_io(nc, t, dbg=0):
    def inp(name, shape, dtype=F32):
        t[name] = nc.dram_tensor(name, list(shape), dtype, kind="ExternalInput").ap()
    inp("xin", (D, LC))
    for nm in ("a_Wq", "a_Wk", "a_Wv", "a_Wo", "b_Wq", "b_Wk", "b_Wv", "b_Wo"):
        inp(nm, (D, D))
    inp("a_W1", (D, FF)); inp("f_W1", (D, FF))
    inp("a_W2", (FF, D)); inp("f_W2", (FF, D))
    inp("cpk", (128, 272))
    for nm in ("a_bq_r", "a_bk_r", "a_bv_r", "b_bv_r"):
        inp(nm, (1, D))
    inp("n_pk", (128, 768))
    inp("role_mask", (128, 1))
    inp("mb", (4 * 32 * 128, 512), BF16)

    t["xout"] = nc.dram_tensor("xout", [D, LC], F32, kind="ExternalOutput").ap()
    for i in range(dbg):
        t[f"dbg{i}"] = nc.dram_tensor(f"dbg{i}", [D, LC], F32, kind="ExternalOutput").ap()

    t["c_ones_col_f"] = nc.inline_tensor(np.ones((128, 1), np.float32), name="c_ones_col_f").ap()
    t["c_ones_col_bf"] = nc.inline_tensor(np.ones((128, 1), ml_dtypes.bfloat16), name="c_ones_col_bf").ap()
    t["c_ones_row_f"] = nc.inline_tensor(np.ones((1, 128), np.float32), name="c_ones_row_f").ap()
    t["c_ones_row_bf"] = nc.inline_tensor(np.ones((1, 128), ml_dtypes.bfloat16), name="c_ones_row_bf").ap()
    t["c_ident_bf"] = nc.inline_tensor(np.eye(128, dtype=ml_dtypes.bfloat16), name="c_ident_bf").ap()
    t["c_ident_f"] = nc.inline_tensor(np.eye(128, dtype=np.float32), name="c_ident_f").ap()
    u = (np.arange(128)[:, None] <= np.arange(128)[None, :]).astype(np.float32)
    t["c_umask_f"] = nc.inline_tensor(u, name="c_umask_f").ap()
    t["c_halfpi"] = nc.inline_tensor(np.full((128, 1), HALF_PI, np.float32), name="c_halfpi").ap()
    t["c_ones128_f"] = nc.inline_tensor(np.ones((128, 128), np.float32), name="c_ones128_f").ap()
    t["c_ones128_bf"] = nc.inline_tensor(np.ones((128, 128), ml_dtypes.bfloat16), name="c_ones128_bf").ap()

    for nm in ("a_Wq", "a_Wk", "a_Wv", "a_Wo", "b_Wq", "b_Wk", "b_Wv", "b_Wo"):
        t[nm + "_bf"] = nc.dram_tensor(nm + "_bf", [D, D], BF16, kind="Internal").ap()
    for nm in ("a_W1", "f_W1"):
        t[nm + "_bf"] = nc.dram_tensor(nm + "_bf", [D, FF], BF16, kind="Internal").ap()
    for nm in ("a_W2", "f_W2"):
        t[nm + "_bf"] = nc.dram_tensor(nm + "_bf", [FF, D], BF16, kind="Internal").ap()

    t["cc_a_in"] = nc.dram_tensor("cc_a_in", [128, 552], F32, kind="Internal").ap()
    t["cc_a_out"] = nc.dram_tensor("cc_a_out", [256, 552], F32, kind="Internal").ap()
    t["ccKa_in"] = nc.dram_tensor("ccKa_in", [1024, 1536], BF16, kind="Internal").ap()
    t["ccKa_out"] = nc.dram_tensor("ccKa_out", [2048, 1536], BF16, kind="Internal").ap()
    t["ccVa_in"] = nc.dram_tensor("ccVa_in", [768, 2048], BF16, kind="Internal").ap()
    t["ccVa_out"] = nc.dram_tensor("ccVa_out", [1536, 2048], BF16, kind="Internal").ap()
    t["ccKb_in"] = nc.dram_tensor("ccKb_in", [1024, 512], BF16, kind="Internal").ap()
    t["ccKb_out"] = nc.dram_tensor("ccKb_out", [2048, 512], BF16, kind="Internal").ap()
    t["ccVb_in"] = nc.dram_tensor("ccVb_in", [256, 2048], BF16, kind="Internal").ap()
    t["ccVb_out"] = nc.dram_tensor("ccVb_out", [512, 2048], BF16, kind="Internal").ap()
    t["ccN_in"] = nc.dram_tensor("ccN_in", [1, 16], BF16, kind="Internal").ap()
    t["ccN_out"] = nc.dram_tensor("ccN_out", [2, 16], BF16, kind="Internal").ap()


def alloc_sbuf(b, ctx):
    nc, t = b.nc, b.t
    def sb(name, shape, dtype=F32):
        t[name] = ctx.enter_context(nc.sbuf_tensor(name, list(shape), dtype)).ap()
    sb("X", (128, 8 * LC))
    sb("SC0", (128, 8 * CG))
    sb("SC1", (128, 2 * CG))
    sb("BC", (128, 2 * CG))
    sb("WS2", (128, 4 * 2048), BF16)
    sb("ACT", (128, 32 * CG), BF16)
    sb("QF", (128, 8 * CG), BF16)
    sb("KF", (128, 8 * CG), BF16)
    sb("XN", (128, 8 * CG), BF16)
    sb("MIX", (128, 8 * CG), BF16)
    sb("RV", (128, 16))
    sb("RVB", (128, 32), BF16)
    sb("RV2", (128, 16))
    sb("RV2B", (128, 32), BF16)
    sb("KVR", (128, 512))
    sb("KVB", (128, 1024), BF16)
    sb("RSC", (16, 8))
    sb("SAq", (128, 64))
    for nm in ("Qrk", "Qrq", "Qcrk", "Qcrq", "Qs1p", "Qs2p", "Qinv1", "Qsi", "Qso",
               "Qs3p", "Qs4p", "Qsa", "Qe", "Qccs", "Qsc", "Qsasc", "Qw1", "Qw2",
               "Qnw", "Qiw"):
        sb(nm, (16, 128))
    sb("COLS", (128, 128)); sb("COLS2", (128, 64)); sb("COLS2B", (128, 128), BF16)
    sb("JNK", (128, 16))
    sb("VSC", (128, 64), BF16)
    sb("ones_col_f", (128, 1)); sb("ones_col_bf", (128, 1), BF16)
    sb("ones_row_f", (1, 128)); sb("onesr_bf_rows", (128, 128), BF16)
    sb("ident_f", (128, 128))
    sb("umask_f", (128, 128)); sb("ones128_f", (128, 128)); sb("halfpi", (128, 1))
    sb("n_pk_s", (128, 768))
    sb("role_mask_s", (128, 1))
    sb("CPK", (128, 272))
    off = 0
    for nm in C8L + C32L:
        w = 8 if nm in C8L else 32
        t[nm + "_s"] = t["CPK"][:, off:off + w]
        off += w
    sb("BROWA", (128, 2 * D), BF16)
    sb("LNWS", (128, 512))
    sb("LNRB", (128, 1536))
    sb("f_c1", (128, 32)); sb("f_c2", (128, 32))
    sb("MQA", (1, 16)); sb("MQB", (1, 16)); sb("MQC", (1, 16)); sb("MQD", (1, 16))
    sb("MH", (128, 16))
    for i in range(8):
        t[f"p{i}"] = ctx.enter_context(nc.psum_tensor(f"p{i}", [128, 512], F32)).ap()


def AR(t):
    A = t["ACT"]
    return dict(
        AKQ=A[:, 0:2048], AQK=A[:, 2048:4096],
        QT=A[:, 4096:5120], KT=A[:, 5120:6144], VT=A[:, 6144:7168],
        MB=A[:, 0:8192],
        KS=A[:, 8192:10240], VS=A[:, 10240:12416], PT=A[:, 12416:13440],
    )



def cast_phase(b, i):
    if getattr(b, "cast_done", None) is None:
        b.cast_done = set()
    if i in b.cast_done:
        return
    b.cast_done.add(i)
    semname, names = b.cast_phases[i]
    sem = b.sems[semname]
    for nm in names:
        b.GP.dma_start(out=b.t[nm + "_bf"][:, :], in_=b.t[nm][:, :]).inc(sem, 16)


def prologue(b):
    t, SY, VE = b.t, b.SY, b.VE
    ld, ve = b.sems["ld"], b.sems["ve"]
    b.cast_phases = [("cw0", ("a_Wq", "a_Wk", "a_Wv")),
                     ("cw1", ("a_Wo", "a_W1", "a_W2")),
                     ("cw2", ("f_W1", "f_W2")),
                     ("cw3", ("b_Wk", "b_Wv", "b_Wq", "b_Wo"))]
    b.cast_ready = {}
    for semname, names in b.cast_phases:
        for nm in names:
            b.cast_ready[nm] = (b.sems[semname], 16 * len(names))
    for e in range(8):
        SY.dma_start(out=t["X"][:, e * LC:(e + 1) * LC],
                     in_=t["xin"][e * 128:(e + 1) * 128, :]).inc(ld, 16)
    pairs = [("ones_col_f", "c_ones_col_f"), ("ones_col_bf", "c_ones_col_bf"),
             ("ones_row_f", "c_ones_row_f"),
             ("ident_f", "c_ident_f"), ("onesr_bf_rows", "c_ones128_bf"),
             ("umask_f", "c_umask_f"), ("ones128_f", "c_ones128_f"), ("halfpi", "c_halfpi"),
             ("n_pk_s", "n_pk"), ("role_mask_s", "role_mask")]
    for dst, src in pairs:
        b.GP.dma_start(out=t[dst][:, :], in_=t[src][:, :]).inc(ld, 16)
    b.GP.dma_start(out=t["CPK"][:, :], in_=t["cpk"][:, :]).inc(ld, 16)
    brow_slots = {"a_bq_r": ("BROWA", 0, 0), "a_bk_r": ("BROWA", 32, 0),
                  "a_bv_r": ("BROWA", 64, 0), "b_bv_r": ("BROWA", 0, 1024)}
    b.brow_slots = brow_slots
    names4 = ("a_bq_r", "a_bk_r", "a_bv_r", "b_bv_r")
    for i, nm in enumerate(names4):
        SY.dma_start(out=t["SC1"][32 * i:32 * i + 1, 0:D], in_=t[nm][:, :]).inc(ld, 16)
    b.ld_done = ld.c
    VE.wait(ld, b.ld_done)
    for i, nm in enumerate(names4):
        tn, hb_, hc_ = brow_slots[nm]
        VE.tensor_copy(t[tn][hb_:hb_ + 1, hc_:hc_ + D], t["SC1"][32 * i:32 * i + 1, 0:D]).inc(ve)
    VE.memset(t["RV"][:, :], 0.0)
    VE.memset(t["RV2"][:, :], 0.0)
    VE.memset(t["KVR"][:, :], 0.0)
    VE.memset(t["RSC"][:, :], 0.0)
    for nm in ("Qrk", "Qrq", "Qcrk", "Qcrq", "Qs1p", "Qs2p", "Qinv1", "Qsi", "Qso",
               "Qs3p", "Qs4p", "Qsa", "Qe", "Qccs", "Qsc", "Qsasc", "Qw1", "Qw2",
               "Qnw", "Qiw"):
        VE.memset(t[nm][:, :], 0.0)
    VE.memset(t["COLS"][:, :], 0.0)
    VE.drain()
    VE.tensor_copy(t["RSC"][0:1, 0:1], t["RSC"][0:1, 0:1]).inc(ve)
    fence(b)


def quantum_consts(b):
    t, VE, SE = b.t, b.VE, b.SE
    ve, se = b.sems["ve"], b.sems["se"]
    th, b1 = t["f_qp_c_s"][:, :], t["f_b1_c_s"][:, :]
    wr, wi = t["f_qwr_c_s"][:, :], t["f_qwi_c_s"][:, :]
    sa, sbt, sd = t["SAq"][:, 0:32], t["SAq"][:, 32:64], t["COLS2"][:, 32:64]

    def rr_sin(dst):
        VE.drain()
        VE.tensor_copy(sbt, sa)
        for _ in range(9):
            VE.drain()
            VE.tensor_single_scalar(sd, sbt, -0.5, ALU.add)
            VE.drain()
            VE.tensor_single_scalar(sd, sd, 1e30, ALU.mult)
            VE.drain()
            VE.tensor_scalar_min(sd, sd, 1.0)
            VE.drain()
            VE.tensor_scalar_max(sd, sd, 0.0)
            VE.drain()
            VE.tensor_sub(sbt, sbt, sd)
            VE.drain()
            VE.tensor_single_scalar(sd, sbt, 0.5, ALU.add)
            VE.drain()
            VE.tensor_single_scalar(sd, sd, -1e30, ALU.mult)
            VE.drain()
            VE.tensor_scalar_min(sd, sd, 1.0)
            VE.drain()
            VE.tensor_scalar_max(sd, sd, 0.0)
            VE.drain()
            v = VE.tensor_add(sbt, sbt, sd).inc(ve)
        SE.wait(ve, v)
        return SE.activation(dst, sbt, AF.Sin, scale=TWO_PI)

    VE.tensor_single_scalar(sa, th, 1.0 / TWO_PI, ALU.mult)
    v = rr_sin(t["COLS"][:, 0:32]).inc(se)
    VE.wait(se, v)
    ve_spacer(b)
    VE.tensor_single_scalar(sa, th, 1.0 / TWO_PI, ALU.mult)
    VE.drain()
    VE.tensor_single_scalar(sa, sa, 0.25, ALU.add)
    VE.drain()
    v = rr_sin(t["COLS"][:, 32:64]).inc(se)
    VE.wait(se, v)
    ve_spacer(b)
    A = t["COLS"][:, 64:96]
    VE.tensor_mul(sa, t["COLS"][:, 32:64], wr)
    VE.tensor_mul(sd, t["COLS"][:, 0:32], wi)
    VE.drain()
    VE.tensor_add(sa, sa, sd)
    VE.drain()
    VE.tensor_single_scalar(sa, sa, 0.1, ALU.mult)
    VE.drain()
    VE.tensor_mul(A, sa, wi)
    VE.drain()
    VE.tensor_single_scalar(sa, th, 1.0 / TWO_PI, ALU.mult)
    VE.tensor_single_scalar(sd, b1, 0.1 / TWO_PI, ALU.mult)
    VE.drain()
    VE.tensor_add(sa, sa, sd)
    VE.drain()
    v = rr_sin(t["COLS"][:, 96:128]).inc(se)
    VE.wait(se, v)
    ve_spacer(b)
    VE.tensor_single_scalar(sa, th, 1.0 / TWO_PI, ALU.mult)
    VE.tensor_single_scalar(sd, b1, 0.1 / TWO_PI, ALU.mult)
    VE.drain()
    VE.tensor_add(sa, sa, sd)
    VE.drain()
    VE.tensor_single_scalar(sa, sa, 0.25, ALU.add)
    VE.drain()
    v = rr_sin(t["COLS2"][:, 0:32]).inc(se)
    VE.wait(se, v)
    ve_spacer(b)
    VE.tensor_mul(t["f_c1"][:, :], A, t["COLS"][:, 96:128])
    VE.tensor_mul(t["f_c2"][:, :], A, t["COLS2"][:, 0:32]).inc(ve)
    fence(b)


def ln_rows(b, src_tile, persist_base=None):
    t, TE, VE, SE = b.t, b.TE, b.VE, b.SE
    te, ve, se = b.sems["te"], b.sems["ve"], b.sems["se"]
    sync_to(b, SE, "vector")
    sync_to(b, TE, "vector")
    rel = []
    for e in range(8):
        sq = t["SC1"][:, (e % 2) * CG:(e % 2) * CG + CG]
        if e >= 2:
            SE.wait(te, rel[e - 2])
        sqv = SE.activation(sq, src_tile(e), AF.Square).inc(se)
        TE.wait(se, sqv)
        TE.matmul(t["p4"][0:1, :], t["ones_col_f"][:, :], src_tile(e), start=(e == 0), stop=(e == 7))
        rel.append(TE.matmul(t["p5"][0:1, :], t["ones_col_f"][:, :], sq, start=(e == 0), stop=(e == 7)).inc(te))
    mm_done = b.sems["te"].c
    m = t["LNWS"][0:1, :]
    r = t["LNWS"][32:33, :]
    tmp = t["LNWS"][64:65, :]
    VE.wait(te, mm_done)
    VE.tensor_single_scalar(m, t["p4"][0:1, :], 1.0 / D, ALU.mult)
    VE.drain()
    VE.tensor_mul(tmp, m, m)
    VE.drain()
    VE.scalar_tensor_tensor(r, t["p5"][0:1, :], 1.0 / D, tmp, ALU.mult, ALU.subtract)
    VE.drain()
    v = VE.tensor_single_scalar(r, r, LNEPS, ALU.add).inc(ve)
    b.bank_rel["p4"] = (b.sems["ve"], v)
    b.bank_rel["p5"] = (b.sems["ve"], v)
    SE.wait(ve, v)
    v = SE.activation(r, r, AF.Sqrt).inc(se)
    VE.wait(se, v)
    ve_spacer(b)
    out = VE.reciprocal(r, r)
    VE.drain()
    if persist_base is not None:
        rb, rc, mb2, mc2 = _lnr_slot(persist_base // 32)
        VE.tensor_copy(t["LNRB"][mb2:mb2 + 1, mc2:mc2 + 512], m)
        out = VE.tensor_copy(t["LNRB"][rb:rb + 1, rc:rc + 512], r)
    return out.inc(ve)


def ln_bcast(b, ready_ve, persist_base=None):
    t, TE, VE = b.t, b.TE, b.VE
    te, ve = b.sems["te"], b.sems["ve"]
    if persist_base is not None:
        rb, rc, mb2, mc2 = _lnr_slot(persist_base // 32)
        VE.tensor_copy(t["LNWS"][0:1, :], t["LNRB"][mb2:mb2 + 1, mc2:mc2 + 512])
        v0 = VE.tensor_copy(t["LNWS"][32:33, :], t["LNRB"][rb:rb + 1, rc:rc + 512]).inc(ve)
        ready_ve = max(ready_ve, v0)
    m = t["LNWS"][0:1, :]
    r = t["LNWS"][32:33, :]
    bank_wait(b, TE, "p6")
    bank_wait(b, TE, "p7")
    TE.wait(ve, ready_ve)
    TE.matmul(t["p6"][:, :], t["ones128_f"][32:33, :], r, start=True, stop=True)
    v = TE.matmul(t["p7"][:, :], t["ones_row_f"][:, :], m, start=True, stop=True).inc(te)
    VE.wait(te, v)
    VE.tensor_copy(t["BC"][:, 0:512], t["p6"][:, :])
    VE.drain()
    out = VE.scalar_tensor_tensor(t["BC"][:, 512:1024], t["p7"][:, :], -1.0,
                                  t["BC"][:, 0:512], ALU.mult, ALU.mult).inc(ve)
    b.bank_rel["p6"] = (b.sems["ve"], out)
    b.bank_rel["p7"] = (b.sems["ve"], out)
    return out


def apply_ln(b, src_tile, dst_tile, bc_ready, gcol=None, bcol=None):
    t, VE = b.t, b.VE
    ve = b.sems["ve"]
    VE.wait(ve, bc_ready)
    last = None
    for e0 in range(0, 8, 2):
        a_ = t["SC1"][:, 0:CG]
        b_ = t["SC1"][:, CG:2 * CG]
        VE.tensor_mul(a_, src_tile(e0), t["BC"][:, 0:512])
        VE.tensor_mul(b_, src_tile(e0 + 1), t["BC"][:, 0:512])
        if gcol is None:
            VE.tensor_add(dst_tile(e0), a_, t["BC"][:, 512:1024])
            last = VE.tensor_add(dst_tile(e0 + 1), b_, t["BC"][:, 512:1024])
        else:
            VE.tensor_add(a_, a_, t["BC"][:, 512:1024])
            VE.tensor_add(b_, b_, t["BC"][:, 512:1024])
            VE.tensor_scalar(dst_tile(e0), a_, t[gcol][:, e0:e0 + 1], t[bcol][:, e0:e0 + 1],
                             ALU.mult, ALU.add)
            last = VE.tensor_scalar(dst_tile(e0 + 1), b_, t[gcol][:, e0 + 1:e0 + 2], t[bcol][:, e0 + 1:e0 + 2],
                                    ALU.mult, ALU.add)
    return last.inc(ve)


class WStream:

    def __init__(self, b, buf_name, slot_w, nslots=2, semgroup=None):
        self.b = b
        self.buf = b.t[buf_name]
        self.slot_w = slot_w
        self.nslots = nslots
        self.n = 0
        self.loads = []
        self.rels = []
        self.ssems = [b.sems[f"{semgroup}{i}"] for i in range(nslots)]

    def load(self, dram_ap, rearr=None, inner=None, ready=None):
        b = self.b
        if ready is not None:
            b.GP.wait(ready[0], ready[1])
        par = self.n % self.nslots
        if self.n >= self.nslots:
            s, v = self.rels[self.n - self.nslots]
            assert s is not None, "WStream slot reused before release recorded"
            b.GP.wait(s, v)
        dst = self.buf[:, par * self.slot_w:(par + 1) * self.slot_w]
        if rearr is None:
            src = dram_ap
            dst_v = dst
        else:
            src = dram_ap.rearrange(rearr[0], **rearr[1])
            dst_v = dst.rearrange("p (c i) -> p c i", i=inner)
        ssem = self.ssems[par]
        v = b.GP.dma_start(out=dst_v, in_=src).inc(ssem, 16)
        self.loads.append((ssem, v))
        self.rels.append(None)
        self.n += 1
        return self.n - 1, dst

    def wait_on(self, es, idx):
        s, v = self.loads[idx]
        es.wait(s, v)

    def release(self, idx, sem, val):
        if self.rels[idx] is None or self.rels[idx][0] is not sem or self.rels[idx][1] < val:
            self.rels[idx] = (sem, val)




def sdma(b, ES_, sem, out, in_):
    ES_.wait(sem, sem.c)
    return ES_.dma_start(out=out, in_=in_).inc(sem, 16)


def _lnr_slot(tt):
    if tt < 3:
        return (32 * tt, 0, 32 * tt, 512)
    return (0, 1024, 32, 1024)


def ve_spacer(b):
    b.VE.tensor_copy(b.t["JNK"][:, :], b.t["ones128_f"][:, 0:16])


def bank_wait(b, es, bank_name):
    s = b.bank_rel.get(bank_name)
    if s is not None:
        es.wait(s[0], s[1])


def proj_feat(b, ws, wname, xn_tile, out_cb, bias_col=None, sigmoid=False,
              banks=("p0", "p1"), rhs_w=CG, widxs=None, release=True):
    t = b.t
    TE, SE = b.TE, b.SE
    te, se = b.sems["te"], b.sems["se"]
    if widxs is None:
        idxs = []
        for q4 in range(4):
            idxs.append(ws.load(t[wname + "_bf"][:, q4 * 256:(q4 + 1) * 256],
                                rearr=("(c p) f -> p c f", dict(p=128)), inner=256,
                                ready=b.cast_ready[wname]))
    else:
        idxs = widxs
    for ob in range(8):
        q4, obl = ob // 2, ob % 2
        idx, wdst = idxs[q4]
        bname = banks[ob % 2]
        bank = t[bname]
        bank_wait(b, TE, bname)
        ws.wait_on(TE, idx)
        for e in range(8):
            lhs = wdst[:, e * 256 + obl * 128: e * 256 + obl * 128 + 128]
            op = TE.matmul(bank[:, 0:rhs_w], lhs, xn_tile(e), start=(e == 0), stop=(e == 7))
        mmv = op.inc(te)
        if ob % 2 == 1 and release:
            ws.release(idx, b.sems["te"], mmv)
        if callable(out_cb):
            out_cb(ob, bank, mmv)
        else:
            tile_fn = out_cb[0]
            SE.wait(te, mmv)
            func = AF.Sigmoid if sigmoid else AF.Identity
            kw = {}
            if bias_col is not None:
                kw["bias"] = t[bias_col][:, ob:ob + 1]
            ev = SE.activation(tile_fn(ob), bank[:, 0:rhs_w], func, **kw).inc(se)
            b.bank_rel[bname] = (b.sems["se"], ev)


def proj_tok(b, whalf_dsts, whalf_idxs, ws, xn_tile, out_ap, bias_row=None,
             sigmoid=False, banks=("p0", "p1")):
    t = b.t
    TE, SE = b.TE, b.SE
    te, se = b.sems["te"], b.sems["se"]
    for c4 in range(4):
        for q4 in range(4):
            idx = whalf_idxs[q4]
            wdst = whalf_dsts[q4]
            bname = banks[(c4 * 4 + q4) % 2]
            bank = t[bname]
            bank_wait(b, TE, bname)
            ws.wait_on(TE, idx)
            for e in range(8):
                lhs = xn_tile(e)[:, c4 * 128:(c4 + 1) * 128]
                rhs = wdst[:, e * 256:(e + 1) * 256]
                op = TE.matmul(bank[:, 0:256], lhs, rhs, start=(e == 0),
                               stop=(e == 7 and bias_row is None))
            if bias_row is not None:
                tn, hb_, hc_ = b.brow_slots[bias_row]
                op = TE.matmul(bank[:, 0:256], t["onesr_bf_rows"][hb_:hb_ + 1, :],
                               t[tn][hb_:hb_ + 1, hc_ + q4 * 256:hc_ + (q4 + 1) * 256],
                               start=False, stop=True)
            mmv = op.inc(te)
            ws.release(idx, b.sems["te"], mmv)
            SE.wait(te, mmv)
            func = AF.Sigmoid if sigmoid else AF.Identity
            ev = SE.activation(out_ap[:, c4 * 1024 + q4 * 256: c4 * 1024 + (q4 + 1) * 256],
                               bank[:, 0:256], func).inc(se)
            b.bank_rel[bname] = (b.sems["se"], ev)


def emit_ffn(b, wp):
    if wp == "f":
        cast_phase(b, 2)
    t = b.t
    TE, VE, SE = b.TE, b.VE, b.SE
    te, ve, se = b.sems["te"], b.sems["ve"], b.sems["se"]
    ws2 = WStream(b, "WS2", 2048, 4, semgroup="w2s")
    rv0 = ln_rows(b, lambda e: t["X"][:, e * LC: e * LC + CG], persist_base=0)
    for tt in range(NTT):
        Xt = lambda e, tt=tt: t["X"][:, e * LC + tt * CG: e * LC + tt * CG + CG]
        XNe = lambda e: t["XN"][:, e * CG:(e + 1) * CG]
        bcv = ln_bcast(b, rv0 if tt == 0 else b.sems["ve"].c, persist_base=32 * tt)
        xnv = apply_ln(b, Xt, XNe, bcv)
        if getattr(b, "dbg_fence", 0) >= 1:
            fence(b)
        act_v = None
        act_vs = []
        for hb2 in range(16):
            idx, wdst = ws2.load(t[f"{wp}_W1_bf"][:, hb2 * 256:(hb2 + 1) * 256],
                                 rearr=("(c p) h -> p c h", dict(p=128)), inner=256,
                                 ready=b.cast_ready[f"{wp}_W1"])
            for sub in range(2):
                hb = hb2 * 2 + sub
                bname = ("p0", "p1", "p2", "p3")[hb % 4]
                bank = t[bname]
                bank_wait(b, TE, bname)
                if sub == 0:
                    ws2.wait_on(TE, idx)
                if hb == 0:
                    TE.wait(ve, xnv)
                for e in range(8):
                    op = TE.matmul(bank[:, :],
                                   wdst[:, e * 256 + sub * 128: e * 256 + sub * 128 + 128],
                                   XNe(e), start=(e == 0), stop=(e == 7))
                mmv = op.inc(te)
                if sub == 1:
                    ws2.release(idx, b.sems["te"], mmv)
                par = hb % 4
                Cs = t["SC0"][:, par * 1024: par * 1024 + 512]
                Ss = t["SC0"][:, par * 1024 + 512: par * 1024 + 1024]
                SE.wait(te, mmv)
                if hb >= 4:
                    SE.wait(ve, act_vs[hb - 4])
                SE.activation(Cs, bank[:, :], AF.Sin, scale=0.1, bias=t["halfpi"][:, 0:1])
                sev = SE.activation(Ss, bank[:, :], AF.Sin, scale=0.1).inc(se)
                VE.wait(se, sev)
                ve_spacer(b)
                u1 = t["SC1"][:, (hb % 2) * 512: (hb % 2) * 512 + 512]
                pv = VE.scalar_tensor_tensor(u1, Cs, t["f_c1"][:, hb:hb + 1], bank[:, :],
                                             ALU.mult, ALU.add).inc(ve)
                b.bank_rel[bname] = (b.sems["ve"], pv)
                VE.drain()
                VE.scalar_tensor_tensor(u1, Ss, t["f_c2"][:, hb:hb + 1], u1, ALU.mult, ALU.add)
                VE.drain()
                act_v = VE.tensor_scalar(t["ACT"][:, hb * 512:(hb + 1) * 512], u1,
                                         t[f"{wp}_b1_c_s"][:, hb:hb + 1], 0.0,
                                         ALU.add, ALU.max).inc(ve)
                act_vs.append(act_v)
        if getattr(b, "dbg_fence", 0) >= 2:
            fence(b)
        for ob in range(8):
            bname = "p2" if ob % 2 == 0 else "p3"
            bank = t[bname]
            for half in range(2):
                idx, wdst = ws2.load(
                    t[f"{wp}_W2_bf"][half * 2048:(half + 1) * 2048, ob * 128:(ob + 1) * 128],
                    rearr=("(c p) d -> p c d", dict(p=128)), inner=128,
                    ready=b.cast_ready[f"{wp}_W2"])
                if half == 0:
                    bank_wait(b, TE, bname)
                    TE.wait(ve, act_v)
                ws2.wait_on(TE, idx)
                for c16 in range(16):
                    hb = half * 16 + c16
                    op = TE.matmul(bank[:, :], wdst[:, c16 * 128:(c16 + 1) * 128],
                                   t["ACT"][:, hb * 512:(hb + 1) * 512],
                                   start=(hb == 0), stop=(hb == 31))
                mmv = op.inc(te)
                ws2.release(idx, b.sems["te"], mmv)
            VE.wait(te, mmv)
            xv = VE.scalar_tensor_tensor(Xt(ob), bank[:, :], t[f"{wp}_b2_c_s"][:, ob:ob + 1],
                                         Xt(ob), ALU.add, ALU.add).inc(ve)
            b.bank_rel[bname] = (b.sems["ve"], xv)
            if ob == 3 and tt + 1 < NTT:
                ln_rows(b, lambda e, tn=tt + 1: t["X"][:, e * LC + tn * CG: e * LC + tn * CG + CG],
                        persist_base=32 * (tt + 1))
    fence(b)


def emit_attn_b(b):
    cast_phase(b, 3)
    t = b.t
    TE, VE, SE, GP, SY = b.TE, b.VE, b.SE, b.GP, b.SY
    te, ve, se, gp, ld, cc = (b.sems[k] for k in ("te", "ve", "se", "gp", "ld", "cc"))
    A = t["ACT"]
    KSsl = [A[:, 0:256], A[:, 256:512]]
    VSsl = [A[:, 512:784], A[:, 784:1056]]
    PTsl = [A[:, 1056:1568], A[:, 1568:2080]]
    MBsl = [A[:, 2080:2592], A[:, 2592:3104]]
    ws2 = WStream(b, "WS2", 2048, 4, semgroup="w2s")

    first_norm = [True]
    rv0 = ln_rows(b, lambda e: t["X"][:, e * LC: e * LC + CG], persist_base=0)
    for tt in range(NTT):
        Xt = lambda e, tt=tt: t["X"][:, e * LC + tt * CG: e * LC + tt * CG + CG]
        XNe = lambda e: t["XN"][:, e * CG:(e + 1) * CG]
        if tt > 0:
            SE.wait(ld, ld.c)
            VE.wait(ld, ld.c)
        bcv = ln_bcast(b, rv0 if tt == 0 else b.sems["ve"].c, persist_base=32 * tt)
        xnv = apply_ln(b, Xt, XNe, bcv)
        TE.wait(ve, xnv)

        def k_out(ob, bank, mmv, tt=tt):
            bname = "p0" if ob % 2 == 0 else "p1"
            ktile = t["KF"][:, ob * CG:(ob + 1) * CG]
            SE.wait(te, mmv)
            kv_ = SE.activation(ktile, bank[:, :], AF.Identity,
                                bias=t["b_bk_c_s"][:, ob:ob + 1]).inc(se)
            b.bank_rel[bname] = (b.sems["se"], kv_)
            sq = t["SC1"][:, (ob % 2) * 512:(ob % 2) * 512 + 512]
            sqv = SE.activation(sq, ktile, AF.Square).inc(se)
            bank_wait(b, TE, "p2"); bank_wait(b, TE, "p3")
            TE.wait(se, sqv)
            TE.matmul(t["p2"][0:1, :], t["ones_col_f"][0:64, :], sq[0:64, :], start=True, stop=True)
            nv = TE.matmul(t["p3"][0:1, :], t["ones_col_f"][64:128, :], sq[64:128, :],
                           start=True, stop=True).inc(te)
            VE.wait(te, nv)
            mx0 = t["JNK"][0:1, 0:1]
            mx1 = t["JNK"][0:1, 1:2]
            VE.reduce_max(mx0, t["p2"][0:1, :], AX.X)
            VE.reduce_max(mx1, t["p3"][0:1, :], AX.X)
            h = 2 * ob
            if first_norm[0] and tt == 0:
                VE.tensor_copy(t["MQA"][0:1, h:h + 1], mx0)
                mv = VE.tensor_copy(t["MQA"][0:1, h + 1:h + 2], mx1).inc(ve)
            else:
                VE.tensor_max(t["MQA"][0:1, h:h + 1], t["MQA"][0:1, h:h + 1], mx0)
                mv = VE.tensor_max(t["MQA"][0:1, h + 1:h + 2], t["MQA"][0:1, h + 1:h + 2], mx1).inc(ve)
            b.bank_rel["p2"] = (b.sems["ve"], mv)
            b.bank_rel["p3"] = (b.sems["ve"], mv)
            SY.wait(se, kv_)
            kdst = (t["ccKa_in"][ob * 128:(ob + 1) * 128, tt * 512:(tt + 1) * 512]
                    if tt < 3 else t["ccKb_in"][ob * 128:(ob + 1) * 128, 0:512])
            SY.dma_start(out=kdst, in_=ktile).inc(ld, 16)

        proj_feat(b, ws2, "b_Wk", XNe, k_out, banks=("p0", "p1"))
        first_norm[0] = False
        if tt == 3:
            sync_to(b, GP, "vector")
            sdma(b, GP, gp, t["ccN_in"][0:1, 0:16], t["MQA"][0:1, 0:16])
            GP.wait(gp, gp.c)
            b.ccn_v = GP.collective_compute("AllGather", ALU.bypass,
                                          replica_groups=[[0, 1], [2, 3], [4, 5], [6, 7]],
                                          ins=[t["ccN_in"].opt()],
                                          outs=[t["ccN_out"].opt()]).inc(cc, 1)

        widxs = [ws2.load(t["b_Wv_bf"][:, q4 * 256:(q4 + 1) * 256],
                          rearr=("(c p) f -> p c f", dict(p=128)), inner=256,
                          ready=b.cast_ready["b_Wv"]) for q4 in range(4)]
        proj_tok(b, [w[1] for w in widxs], [w[0] for w in widxs], ws2, XNe,
                 t["MIX"], bias_row="b_bv_r", banks=("p2", "p3"))
        for c4 in range(4):
            SY.wait(se, se.c)
            vdst = (t["ccVa_in"][tt * 256 + c4 * 64: tt * 256 + (c4 + 1) * 64, :]
                    if tt < 3 else t["ccVb_in"][c4 * 64:(c4 + 1) * 64, :])
            SY.dma_start(out=vdst.rearrange("r (h f) -> (r h) f", h=2),
                         in_=t["MIX"][:, c4 * 1024:(c4 + 1) * 1024]).inc(ld, 16)
        if tt == 2:
            GP.wait(ld, ld.c)
            GP.collective_compute("AllGather", ALU.bypass,
                                  replica_groups=[[0, 1], [2, 3], [4, 5], [6, 7]],
                                  ins=[t["ccKa_in"].opt()],
                                  outs=[t["ccKa_out"].opt()]).inc(cc, 1)
            b.ccab_v = GP.collective_compute("AllGather", ALU.bypass,
                                           replica_groups=[[0, 1], [2, 3], [4, 5], [6, 7]],
                                           ins=[t["ccVa_in"].opt()],
                                           outs=[t["ccVa_out"].opt()]).inc(cc, 1)
        if tt + 1 < NTT:
            ln_rows(b, lambda e, tn=tt + 1: t["X"][:, e * LC + tn * CG: e * LC + tn * CG + CG],
                    persist_base=32 * (tt + 1))

    bq_pre = [ws2.load(t["b_Wq_bf"][:, q4 * 256:(q4 + 1) * 256],
                       rearr=("(c p) f -> p c f", dict(p=128)), inner=256,
                       ready=b.cast_ready["b_Wq"]) for q4 in range(4)]
    GP.wait(ld, ld.c)
    GP.collective_compute("AllGather", ALU.bypass,
                          replica_groups=[[0, 1], [2, 3], [4, 5], [6, 7]],
                          ins=[t["ccKb_in"].opt()],
                          outs=[t["ccKb_out"].opt()]).inc(cc, 1)
    ccv = GP.collective_compute("AllGather", ALU.bypass,
                                replica_groups=[[0, 1], [2, 3], [4, 5], [6, 7]],
                                ins=[t["ccVb_in"].opt()],
                                outs=[t["ccVb_out"].opt()]).inc(cc, 1)
    b.kvb_ccv = ccv
    for sl in range(2):
        for h4 in range(4):
            VE.memset(VSsl[sl][:, h4 * 68 + 64: h4 * 68 + 65], 1.0)
    VE.drain()
    VE.tensor_copy(t["JNK"][0:1, 0:1], t["JNK"][0:1, 0:1]).inc(ve)

    for iqp in range(NTT):
        Xt = lambda e, iqp=iqp: t["X"][:, e * LC + iqp * CG: e * LC + iqp * CG + CG]
        XNe = lambda e: t["XN"][:, e * CG:(e + 1) * CG]
        QFe = lambda e: t["QF"][:, e * CG:(e + 1) * CG]
        bcv = ln_bcast(b, b.sems["ve"].c, persist_base=32 * iqp)
        xnv = apply_ln(b, Xt, XNe, bcv)
        TE.wait(ve, xnv)

        qnorm_first = [True]

        def q_out(ob, bank, mmv):
            bname = "p0" if ob % 2 == 0 else "p1"
            qtile = QFe(ob)
            SE.wait(te, mmv)
            qv_ = SE.activation(qtile, bank[:, :], AF.Identity,
                                bias=t["b_bq_c_s"][:, ob:ob + 1]).inc(se)
            b.bank_rel[bname] = (b.sems["se"], qv_)
            sq = t["SC1"][:, (ob % 2) * 512:(ob % 2) * 512 + 512]
            sqv = SE.activation(sq, qtile, AF.Square).inc(se)
            bank_wait(b, TE, "p2"); bank_wait(b, TE, "p3")
            TE.wait(se, sqv)
            TE.matmul(t["p2"][0:1, :], t["ones_col_f"][0:64, :], sq[0:64, :], start=True, stop=True)
            nv = TE.matmul(t["p3"][0:1, :], t["ones_col_f"][64:128, :], sq[64:128, :],
                           start=True, stop=True).inc(te)
            VE.wait(te, nv)
            mx0 = t["JNK"][0:1, 0:1]
            mx1 = t["JNK"][0:1, 1:2]
            VE.reduce_max(mx0, t["p2"][0:1, :], AX.X)
            VE.reduce_max(mx1, t["p3"][0:1, :], AX.X)
            h = 2 * ob
            if qnorm_first[0] or iqp == 0:
                VE.tensor_copy(t["MQB"][0:1, h:h + 1], mx0)
                mv = VE.tensor_copy(t["MQB"][0:1, h + 1:h + 2], mx1).inc(ve)
                qnorm_first[0] = False
            else:
                VE.tensor_max(t["MQB"][0:1, h:h + 1], t["MQB"][0:1, h:h + 1], mx0)
                mv = VE.tensor_max(t["MQB"][0:1, h + 1:h + 2], t["MQB"][0:1, h + 1:h + 2], mx1).inc(ve)
            b.bank_rel["p2"] = (b.sems["ve"], mv)
            b.bank_rel["p3"] = (b.sems["ve"], mv)

        proj_feat(b, ws2, "b_Wq", XNe, q_out, banks=("p0", "p1"),
                  widxs=(bq_pre if iqp == 0 else None))
        qnorm_first[0] = False
        if iqp == 0:
            GP.wait(cc, b.ccn_v)
            GP.dma_start(out=t["LNWS"][64:65, 0:16], in_=t["ccN_out"][0:1, 0:16]).inc(gp, 16)
            GP.dma_start(out=t["LNWS"][64:65, 16:32], in_=t["ccN_out"][1:2, 0:16]).inc(gp, 16)
            VE.wait(gp, gp.c)
            VE.tensor_max(t["MQD"][0:1, 0:16], t["LNWS"][64:65, 0:16],
                          t["LNWS"][64:65, 16:32]).inc(ve)
            SY.wait(cc, b.ccab_v)
            b.kvb_waited = False

        mrow = t["LNWS"][32:33, 0:16]
        VE.wait(ve, ve.c)
        mqv = VE.tensor_mul(mrow, t["MQB"][0:1, 0:16], t["MQD"][0:1, 0:16]).inc(ve)
        SE.wait(ve, mqv)
        sev = SE.activation(mrow, mrow, AF.Sqrt).inc(se)
        VE.wait(se, sev)
        ve_spacer(b)
        mqv = VE.tensor_single_scalar(mrow, mrow, -1.02 / 8.0, ALU.mult).inc(ve)
        bank_wait(b, TE, "p6")
        TE.wait(ve, mqv)
        bv_ = TE.matmul(t["p6"][:, 0:16], t["ones128_f"][32:33, :], mrow, start=True, stop=True).inc(te)
        VE.wait(te, bv_)
        mhv = VE.tensor_copy(t["MH"][:, :], t["p6"][:, 0:16]).inc(ve)
        b.bank_rel["p6"] = (b.sems["ve"], mhv)
        SE.wait(ve, mhv)

        KBMAX = 20 + iqp * 4
        for hs4 in range(4):
            obank = [f"p{2 + i}" for i in range(4)]
            for kb in range(KBMAX):
                par = kb % 2
                rank, kb16 = kb // 16, kb % 16
                need_mask = kb >= iqp * 4
                rel = b.flash_rel.get(par)
                if rel is not None:
                    SY.wait(rel[0], rel[1])
                kvsem = b.sems[f"kv{par}"]
                mbsem = b.sems[f"mb{par}"]
                if kb16 < 12:
                    Ksrc = t["ccKa_out"][rank * 1024 + hs4 * 256: rank * 1024 + (hs4 + 1) * 256,
                                         kb16 * 128:(kb16 + 1) * 128]
                    Vsrc = t["ccVa_out"][rank * 768 + kb16 * 64: rank * 768 + (kb16 + 1) * 64, :]
                else:
                    if not b.kvb_waited:
                        SY.wait(cc, b.kvb_ccv)
                        b.kvb_waited = True
                    kk = kb16 - 12
                    Ksrc = t["ccKb_out"][rank * 1024 + hs4 * 256: rank * 1024 + (hs4 + 1) * 256,
                                         kk * 128:(kk + 1) * 128]
                    Vsrc = t["ccVb_out"][rank * 256 + kk * 64: rank * 256 + (kk + 1) * 64, :]
                sdma(b, SY, kvsem,
                     KSsl[par].rearrange("p (c tk) -> p c tk", tk=128),
                     Ksrc.rearrange("(c p) tk -> p c tk", p=128))
                kvv = sdma(b, SY, kvsem,
                           VSsl[par].rearrange("p (h w) -> p h w", w=68)[:, :, 0:64],
                           Vsrc.rearrange("r (hh hf) -> (r hh) hf", hh=2)
                               .rearrange("p (h f) -> p h f", f=64)[:, hs4 * 4:(hs4 + 1) * 4, :])
                mbv = None
                if need_mask:
                    mrow0 = (iqp * 32 + kb) * 128
                    mbv = sdma(b, SY, mbsem, MBsl[par], t["mb"][mrow0:mrow0 + 128, :])

                pend = {}

                def emit_st(h4, kb=kb, par=par, hs4=hs4, need_mask=need_mask,
                            kvv=kvv, mbv=mbv):
                    h = hs4 * 4 + h4
                    sbname = ("p6", "p7", "p0", "p1")[h4]
                    sbank = t[sbname]
                    bank_wait(b, TE, sbname)
                    if h4 == 0:
                        TE.wait(kvsem, kvv)
                        if need_mask:
                            TE.wait(mbsem, mbv)
                    lhs = KSsl[par][64 * (h4 % 2):64 * (h4 % 2) + 64,
                                    (h4 // 2) * 128:(h4 // 2) * 128 + 128]
                    rhs = t["QF"][64 * (h % 2):64 * (h % 2) + 64, (h // 2) * CG:(h // 2) * CG + CG]
                    stv = TE.matmul(sbank[:, :], lhs, rhs, start=True, stop=True).inc(te)
                    SE.wait(te, stv)
                    pt = PTsl[h4 % 2]
                    sev = SE.activation(pt, sbank[:, :], AF.Exp, scale=0.125,
                                        bias=t["MH"][:, h:h + 1]).inc(se)
                    b.bank_rel[sbname] = (b.sems["se"], sev)
                    if need_mask:
                        VE.wait(se, sev)
                        ve_spacer(b)
                        mkv = VE.tensor_mul(pt, pt, MBsl[par]).inc(ve)
                        pend[h4] = (b.sems["ve"], mkv, pt)
                    else:
                        pend[h4] = (b.sems["se"], sev, pt)

                def emit_av(h4, kb=kb, par=par):
                    sem, v, pt = pend[h4]
                    obname = obank[h4]
                    if kb == 0:
                        bank_wait(b, TE, obname)
                    TE.wait(sem, v)
                    return TE.matmul(t[obname][0:65, :], VSsl[par][:, h4 * 68: h4 * 68 + 65],
                                     pt, start=(kb == 0), stop=(kb == KBMAX - 1)).inc(te)

                emit_st(0)
                emit_st(1)
                emit_av(0)
                emit_st(2)
                emit_av(1)
                emit_st(3)
                emit_av(2)
                last_read = emit_av(3)
                b.flash_rel[par] = (b.sems["te"], last_read)
            for h4 in range(4):
                h = hs4 * 4 + h4
                obname = obank[h4]
                obk = t[obname]
                rec = t["LNWS"][32:33, 0:512]
                VE.wait(te, b.sems["te"].c)
                rcv = VE.reciprocal(rec, obk[64:65, :]).inc(ve)
                VE.drain()
                bank_wait(b, TE, "p6")
                TE.wait(ve, rcv)
                bcv2 = TE.matmul(t["p6"][0:64, :], t["ones128_f"][32:33, 0:64],
                                 rec, start=True, stop=True).inc(te)
                VE.wait(te, bcv2)
                VE.tensor_copy(t["SC1"][0:64, 512:1024], t["p6"][0:64, :])
                dst = t["XN"][64 * (h % 2):64 * (h % 2) + 64, (h // 2) * CG:(h // 2) * CG + CG]
                nv = VE.tensor_mul(dst, obk[0:64, :], t["SC1"][0:64, 512:1024]).inc(ve)
                b.bank_rel["p6"] = (b.sems["ve"], nv)
                b.bank_rel[obname] = (b.sems["ve"], nv)

        def wo_out(ob, bank, mmv, iqp=iqp):
            bname = "p0" if ob % 2 == 0 else "p1"
            VE.wait(te, mmv)
            xsl = t["X"][:, ob * LC + iqp * CG: ob * LC + iqp * CG + CG]
            xv = VE.scalar_tensor_tensor(xsl, bank[:, :], t["b_bo_c_s"][:, ob:ob + 1],
                                         xsl, ALU.add, ALU.add).inc(ve)
            b.bank_rel[bname] = (b.sems["ve"], xv)

        VE.wait(ve, ve.c)
        proj_feat(b, ws2, "b_Wo", XNe, wo_out, banks=("p0", "p1"))
    fence(b)


def nsl(b, c):
    t = b.t
    r0, c0 = 32 * (c % 3), 128 * (c // 3)
    return (t["n_pk_s"][r0:r0 + 16, c0:c0 + 128], None)


def mixer_chunk(b, pss, cg, c4):
    t = b.t
    TE, VE, SE = b.TE, b.VE, b.SE
    te, ve, se = b.sems["te"], b.sems["ve"], b.sems["se"]
    A = t["ACT"]
    AKQ, AQK = A[:, 0:2048], A[:, 2048:4096]
    QT, KT, VT = A[:, 4096:8192], A[:, 8192:12288], A[:, 12288:16384]
    c = cg * 4 + c4
    r0, c0 = 32 * (c % 3), 128 * (c // 3)

    def qf_h(h):
        return t["QF"][64 * (h % 2):64 * (h % 2) + 64, (h // 2) * CG + c4 * 128:(h // 2) * CG + c4 * 128 + 128]

    def kf_h(h):
        return t["KF"][64 * (h % 2):64 * (h % 2) + 64, (h // 2) * CG + c4 * 128:(h // 2) * CG + c4 * 128 + 128]

    def qt_h(h):
        return QT[:, c4 * 1024 + 64 * h: c4 * 1024 + 64 * h + 64]

    def kt_h(h):
        return KT[:, c4 * 1024 + 64 * h: c4 * 1024 + 64 * h + 64]

    def vt_h(h):
        return VT[:, c4 * 1024 + 64 * h: c4 * 1024 + 64 * h + 64]

    onebf = t["ones_col_bf"]
    Q = lambda nm: t[nm][0:16, :]

    VE.tensor_copy(t["Qnw"][0:16, :], t["n_pk_s"][r0:r0 + 16, c0:c0 + 128])

    sync_to(b, b.VE, "tensor")
    VE.tensor_copy(t["RVB"][:, 0:16], t["RV"][:, :])
    VE.tensor_copy(t["RV2B"][:, 0:16], t["RV2"][:, :])
    VE.tensor_copy(t["KVB"][:, 0:512], t["KVR"][:, :])
    VE.tensor_sub(t["SC1"][:, 0:16], t["RV"][:, :], t["RVB"][:, 0:16])
    VE.tensor_sub(t["SC1"][:, 16:32], t["RV2"][:, :], t["RV2B"][:, 0:16])
    VE.tensor_sub(t["SC1"][:, 32:544], t["KVR"][:, :], t["KVB"][:, 0:512])
    VE.tensor_copy(t["RVB"][:, 16:32], t["SC1"][:, 0:16])
    VE.tensor_copy(t["RV2B"][:, 16:32], t["SC1"][:, 16:32])
    VE.tensor_copy(t["KVB"][:, 512:1024], t["SC1"][:, 32:544])
    VE.reciprocal(t["Qiw"][0:16, :], t["Qnw"][0:16, :])
    VE.drain()

    sync_to(b, b.TE, "vector")
    for h in range(H):
        bname = "p0" if h % 2 == 0 else "p1"
        bank = t[bname]
        bank_wait(b, TE, bname)
        TE.matmul(bank[:, 0:128], kf_h(h), qf_h(h), start=True, stop=True)
        mv = TE.matmul(bank[:, 128:256], qf_h(h), kf_h(h), start=False, stop=True,
                       skip_group_check=True).inc(te)
        VE.wait(te, mv)
        VE.tensor_mul(AKQ[:, h * 128:(h + 1) * 128], bank[:, 0:128], t["umask_f"][:, :])
        ev = VE.tensor_mul(AQK[:, h * 128:(h + 1) * 128], bank[:, 128:256], t["umask_f"][:, :]).inc(ve)
        b.bank_rel[bname] = (b.sems["ve"], ev)

    sync_to(b, b.TE, "vector")
    bank_wait(b, TE, "p2")
    p2 = t["p2"]
    first = True
    for h in range(H):
        hp, par = h // 2, h % 2
        ob = 64 * par
        TE.matmul(p2[:, h:h + 1], AKQ[:, h * 128:(h + 1) * 128], onebf[:, :],
                  start=first, stop=False, skip_group_check=True)
        first = False
        TE.matmul(p2[:, h:h + 1], qf_h(h), t["RVB"][ob:ob + 64, hp:hp + 1],
                  start=False, stop=False, skip_group_check=True)
        TE.matmul(p2[:, h:h + 1], qf_h(h), t["RVB"][ob:ob + 64, 16 + hp:17 + hp],
                  start=False, stop=True, skip_group_check=True)
        TE.matmul(p2[:, 32 + h:33 + h], AQK[:, h * 128:(h + 1) * 128], onebf[:, :],
                  start=False, stop=False, skip_group_check=True)
        TE.matmul(p2[:, 32 + h:33 + h], kf_h(h), t["RVB"][ob:ob + 64, 8 + hp:9 + hp],
                  start=False, stop=False, skip_group_check=True)
        TE.matmul(p2[:, 32 + h:33 + h], kf_h(h), t["RVB"][ob:ob + 64, 24 + hp:25 + hp],
                  start=False, stop=True, skip_group_check=True)
        TE.matmul(p2[:, 64 + h:65 + h], kf_h(h), onebf[ob:ob + 64, :],
                  start=False, stop=True, skip_group_check=True)
        TE.matmul(p2[:, 96 + h:97 + h], qf_h(h), onebf[ob:ob + 64, :],
                  start=False, stop=True, skip_group_check=True)
    bank_wait(b, TE, "p3")
    firstq = True
    for q, TT_ in ((0, KT), (1, QT)):
        for hp in range(8):
            lhs = TT_[:, c4 * 1024 + hp * 128: c4 * 1024 + (hp + 1) * 128]
            TE.matmul(t["p3"][:, q * 8 + hp: q * 8 + hp + 1], lhs, onebf[:, :],
                      start=firstq, stop=True, skip_group_check=True)
            firstq = False

    sync_to(b, b.VE, "tensor")
    for q in range(4):
        VE.tensor_copy(t["COLS"][:, 32 * q:32 * q + 16], p2[:, 32 * q:32 * q + 16])
    b.bank_rel["p2"] = (b.sems["ve"], bump(b, "vector"))
    sync_to(b, b.TE, "vector")
    bank_wait(b, TE, "p4")
    TE.transpose(t["p4"][0:64, 0:128], t["COLS"][:, 0:64], t["ident_f"][:, :])
    TE.transpose(t["p4"][0:64, 128:256], t["COLS"][:, 64:128], t["ident_f"][:, :])
    sync_to(b, b.VE, "tensor")
    VE.tensor_copy(Q("Qs1p"), t["p4"][0:16, 0:128])
    VE.tensor_copy(Q("Qs2p"), t["p4"][32:48, 0:128])
    VE.tensor_copy(Q("Qrk"), t["p4"][0:16, 128:256])
    VE.tensor_copy(Q("Qrq"), t["p4"][32:48, 128:256])
    b.bank_rel["p4"] = (b.sems["ve"], bump(b, "vector"))
    VE.tensor_add(t["RV"][:, :], t["RV"][:, :], t["p3"][:, 0:16])
    b.bank_rel["p3"] = (b.sems["ve"], bump(b, "vector"))

    VE.tensor_tensor_scan(Q("Qcrk"), t["ones128_f"][0:16, :], Q("Qrk"),
                          t["RSC"][0:16, 0:1], ALU.mult, ALU.add)
    VE.tensor_tensor_scan(Q("Qcrq"), t["ones128_f"][0:16, :], Q("Qrq"),
                          t["RSC"][0:16, 1:2], ALU.mult, ALU.add)
    VE.tensor_copy(t["RSC"][0:16, 0:1], t["Qcrk"][0:16, 127:128])
    VE.tensor_copy(t["RSC"][0:16, 1:2], t["Qcrq"][0:16, 127:128])
    VE.tensor_add(Q("Qw1"), Q("Qcrk"), Q("Qrq"))
    VE.tensor_add(Q("Qsc"), Q("Qcrq"), Q("Qrk"))
    VE.scalar_tensor_tensor(Q("Qw2"), Q("Qw1"), EPS, Q("Qs1p"), ALU.mult, ALU.add)
    VE.scalar_tensor_tensor(Q("Qsasc"), Q("Qsc"), EPS, Q("Qs2p"), ALU.mult, ALU.add)
    VE.tensor_single_scalar(Q("Qw1"), Q("Qw2"), EPS2_64, ALU.add)
    VE.tensor_single_scalar(Q("Qsc"), Q("Qsasc"), EPS2_64, ALU.add)
    VE.reciprocal(Q("Qinv1"), Q("Qw1"))
    VE.drain()
    VE.reciprocal(Q("Qw2"), Q("Qsc"))
    VE.drain()
    VE.tensor_mul(Q("Qsi"), Q("Qinv1"), Q("Qnw"))
    VE.tensor_mul(Q("Qso"), Q("Qw2"), Q("Qnw"))

    sync_to(b, b.TE, "vector")
    bank_wait(b, TE, "p4")
    TE.transpose(t["p4"][:, 0:16], Q("Qsi"), t["ident_f"][0:16, 0:16])
    TE.transpose(t["p4"][:, 32:48], Q("Qso"), t["ident_f"][0:16, 0:16])
    sync_to(b, b.VE, "tensor")
    VE.tensor_copy(t["COLS2"][:, 0:16], t["p4"][:, 0:16])
    VE.tensor_copy(t["COLS2"][:, 32:48], t["p4"][:, 32:48])
    b.bank_rel["p4"] = (b.sems["ve"], bump(b, "vector"))
    VE.tensor_copy(t["COLS2B"][:, 0:64], t["COLS2"][:, :])
    VE.drain()
    VE.tensor_sub(t["SC1"][:, 0:64], t["COLS2"][:, :], t["COLS2B"][:, 0:64])
    VE.drain()
    VE.tensor_copy(t["COLS2B"][:, 64:128], t["SC1"][:, 0:64])
    VE.drain()

    sync_to(b, b.TE, "vector")
    bank_wait(b, TE, "p2")
    first = True
    for h in range(H):
        hp, par = h // 2, h % 2
        ob = 64 * par
        TE.matmul(p2[:, h:h + 1], AKQ[:, h * 128:(h + 1) * 128], t["COLS2B"][:, 32 + h:33 + h],
                  start=first, stop=False, skip_group_check=True)
        first = False
        TE.matmul(p2[:, h:h + 1], AKQ[:, h * 128:(h + 1) * 128], t["COLS2B"][:, 96 + h:97 + h],
                  start=False, stop=False, skip_group_check=True)
        TE.matmul(p2[:, h:h + 1], qf_h(h), t["RV2B"][ob:ob + 64, hp:hp + 1],
                  start=False, stop=False, skip_group_check=True)
        TE.matmul(p2[:, h:h + 1], qf_h(h), t["RV2B"][ob:ob + 64, 16 + hp:17 + hp],
                  start=False, stop=True, skip_group_check=True)
        TE.matmul(p2[:, 32 + h:33 + h], AQK[:, h * 128:(h + 1) * 128], t["COLS2B"][:, h:h + 1],
                  start=False, stop=False, skip_group_check=True)
        TE.matmul(p2[:, 32 + h:33 + h], AQK[:, h * 128:(h + 1) * 128], t["COLS2B"][:, 64 + h:65 + h],
                  start=False, stop=False, skip_group_check=True)
        TE.matmul(p2[:, 32 + h:33 + h], kf_h(h), t["RV2B"][ob:ob + 64, 8 + hp:9 + hp],
                  start=False, stop=False, skip_group_check=True)
        TE.matmul(p2[:, 32 + h:33 + h], kf_h(h), t["RV2B"][ob:ob + 64, 24 + hp:25 + hp],
                  start=False, stop=True, skip_group_check=True)
    bank_wait(b, TE, "p3")
    firstq = True
    for q in range(2):
        for h in range(H):
            hp, par = h // 2, h % 2
            col = 64 + q * 16 + par * 8 + hp
            lhs = kt_h(h) if q == 0 else qt_h(h)
            hicol = t["COLS2B"][:, (32 + h):(33 + h)] if q == 0 else t["COLS2B"][:, h:h + 1]
            locol = t["COLS2B"][:, (96 + h):(97 + h)] if q == 0 else t["COLS2B"][:, 64 + h:65 + h]
            TE.matmul(t["p3"][0:64, col:col + 1], lhs, hicol, start=firstq, stop=False,
                      skip_group_check=True)
            firstq = False
            TE.matmul(t["p3"][0:64, col:col + 1], lhs, locol, start=False, stop=True,
                      skip_group_check=True)

    sync_to(b, b.VE, "tensor")
    VE.tensor_copy(t["COLS"][:, 0:16], p2[:, 0:16])
    VE.tensor_copy(t["COLS"][:, 32:48], p2[:, 32:48])
    b.bank_rel["p2"] = (b.sems["ve"], bump(b, "vector"))
    sync_to(b, b.TE, "vector")
    bank_wait(b, TE, "p4")
    TE.transpose(t["p4"][0:64, 0:128], t["COLS"][:, 0:64], t["ident_f"][:, :])
    sync_to(b, b.VE, "tensor")
    VE.tensor_copy(Q("Qs3p"), t["p4"][0:16, 0:128])
    VE.tensor_copy(Q("Qs4p"), t["p4"][32:48, 0:128])
    b.bank_rel["p4"] = (b.sems["ve"], bump(b, "vector"))
    for q in range(2):
        VE.tensor_add(t["RV2"][0:64, q * 8:(q + 1) * 8], t["RV2"][0:64, q * 8:(q + 1) * 8],
                      t["p3"][0:64, 64 + q * 16: 64 + q * 16 + 8])
        VE.tensor_add(t["RV2"][64:128, q * 8:(q + 1) * 8], t["RV2"][64:128, q * 8:(q + 1) * 8],
                      t["p3"][0:64, 64 + q * 16 + 8: 64 + q * 16 + 16])
    b.bank_rel["p3"] = (b.sems["ve"], bump(b, "vector"))

    VE.tensor_mul(Q("Qw1"), Q("Qrk"), Q("Qso"))
    VE.tensor_mul(Q("Qw2"), Q("Qrq"), Q("Qsi"))
    VE.tensor_tensor_scan(Q("Qcrk"), t["ones128_f"][0:16, :], Q("Qw1"),
                          t["RSC"][0:16, 2:3], ALU.mult, ALU.add)
    VE.tensor_tensor_scan(Q("Qcrq"), t["ones128_f"][0:16, :], Q("Qw2"),
                          t["RSC"][0:16, 3:4], ALU.mult, ALU.add)
    VE.tensor_copy(t["RSC"][0:16, 2:3], t["Qcrk"][0:16, 127:128])
    VE.tensor_copy(t["RSC"][0:16, 3:4], t["Qcrq"][0:16, 127:128])
    VE.tensor_add(Q("Qw1"), Q("Qcrk"), Q("Qrq"))
    VE.tensor_add(Q("Qw2"), Q("Qcrq"), Q("Qrk"))
    VE.scalar_tensor_tensor(Q("Qsc"), Q("Qw1"), EPS, Q("Qs3p"), ALU.mult, ALU.add)
    VE.scalar_tensor_tensor(Q("Qsasc"), Q("Qw2"), EPS, Q("Qs4p"), ALU.mult, ALU.add)
    VE.tensor_single_scalar(Q("Qw1"), Q("Qsc"), EPS2_64, ALU.add)
    VE.tensor_single_scalar(Q("Qw2"), Q("Qsasc"), EPS2_64, ALU.add)
    cs3 = VE.tensor_mul(Q("Qsc"), Q("Qw1"), Q("Qiw")).inc(ve)
    cl0 = VE.tensor_mul(Q("Qsasc"), Q("Qw2"), Q("Qiw")).inc(ve)
    SE.wait(ve, cs3)
    sav = SE.activation(Q("Qsa"), Q("Qsc"), AF.Sigmoid).inc(se)
    VE.tensor_scalar(Q("Qw2"), Q("Qsasc"), 1.0, -1.0, ALU.min, ALU.max)
    cl = bump(b, "vector")
    SE.wait(ve, cl)
    ev_ = SE.activation(Q("Qe"), Q("Qw2"), AF.Exp).inc(se)
    VE.wait(se, ev_)
    ve_spacer(b)
    VE.tensor_tensor_scan(Q("Qccs"), t["ones128_f"][0:16, :], Q("Qe"),
                          t["RSC"][0:16, 4:5], ALU.mult, ALU.add)
    VE.drain()
    VE.reciprocal(Q("Qw1"), Q("Qccs"))
    VE.drain()
    VE.tensor_copy(t["RSC"][0:16, 4:5], t["Qccs"][0:16, 127:128])
    VE.tensor_mul(Q("Qw2"), Q("Qe"), Q("Qw1"))
    VE.wait(se, sav)
    ve_spacer(b)
    VE.tensor_mul(Q("Qsasc"), Q("Qsa"), Q("Qinv1"))
    VE.tensor_mul(Q("Qsc"), Q("Qw2"), Q("Qnw"))

    sync_to(b, b.TE, "vector")
    bank_wait(b, TE, "p4")
    TE.transpose(t["p4"][:, 0:16], Q("Qsc"), t["ident_f"][0:16, 0:16])
    TE.transpose(t["p4"][:, 32:48], Q("Qsasc"), t["ident_f"][0:16, 0:16])
    sync_to(b, b.VE, "tensor")
    VE.tensor_copy(t["COLS2"][:, 0:16], t["p4"][:, 0:16])
    VE.tensor_copy(t["COLS2"][:, 32:48], t["p4"][:, 32:48])
    b.bank_rel["p4"] = (b.sems["ve"], bump(b, "vector"))

    kv_first = True
    for h in range(H):
        hp, par = h // 2, h % 2
        sync_to(b, b.VE, "tensor")
        vsv = VE.tensor_scalar_mul(t["VSC"][:, :], vt_h(h), t["COLS2"][:, h:h + 1]).inc(ve)
        pb = "p5" if par == 0 else "p6"
        if kv_first:
            bank_wait(b, TE, "p5")
            bank_wait(b, TE, "p6")
        TE.wait(ve, vsv)
        TE.matmul(t[pb][0:64, hp * 64:(hp + 1) * 64], kt_h(h), t["VSC"][:, :],
                  start=kv_first or (h == 1), stop=True, skip_group_check=True)
        kv_first = False
        if pss == 1:
            bank_wait(b, TE, "p7")
            TE.matmul(t["p7"][:, 0:64], AKQ[:, h * 128:(h + 1) * 128], t["VSC"][:, :],
                      start=True, stop=False)
            TE.matmul(t["p7"][:, 0:64], qf_h(h), t["KVB"][64 * par:64 * par + 64, hp * 64:(hp + 1) * 64],
                      start=False, stop=False)
            ov = TE.matmul(t["p7"][:, 0:64], qf_h(h),
                           t["KVB"][64 * par:64 * par + 64, 512 + hp * 64: 512 + (hp + 1) * 64],
                           start=False, stop=True).inc(te)
            VE.wait(te, ov)
            mxv = VE.tensor_scalar_mul(t["SC1"][:, 0:64], t["p7"][:, 0:64],
                                       t["COLS2"][:, 32 + h:33 + h]).inc(ve)
            b.bank_rel["p7"] = (b.sems["ve"], mxv)
            sync_to(b, b.TE, "vector")
            bank_wait(b, TE, "p4")
            TE.transpose(t["p4"][0:64, 0:128], t["SC1"][:, 0:64], t["ident_f"][:, :])
            sync_to(b, b.VE, "tensor")
            cpv = VE.tensor_copy(t["MIX"][64 * par:64 * par + 64, hp * CG + c4 * 128: hp * CG + (c4 + 1) * 128],
                                 t["p4"][0:64, 0:128]).inc(ve)
            b.bank_rel["p4"] = (b.sems["ve"], cpv)
    sync_to(b, b.VE, "tensor")
    VE.tensor_add(t["KVR"][0:64, :], t["KVR"][0:64, :], t["p5"][0:64, :])
    VE.tensor_add(t["KVR"][64:128, :], t["KVR"][64:128, :], t["p6"][0:64, :])
    v = bump(b, "vector")
    b.bank_rel["p5"] = (b.sems["ve"], v)
    b.bank_rel["p6"] = (b.sems["ve"], v)


def emit_attn_a(b):
    t = b.t
    TE, VE, SE, GP, SY = b.TE, b.VE, b.SE, b.GP, b.SY
    te, ve, se, gp, ld, cc = (b.sems[k] for k in ("te", "ve", "se", "gp", "ld", "cc"))
    A = t["ACT"]
    QT, KT, VT = A[:, 4096:8192], A[:, 8192:12288], A[:, 12288:16384]
    ws2 = WStream(b, "WS2", 2048, 4, semgroup="w2s")

    for tt in range(NTT):
        Xt = lambda e, tt=tt: t["X"][:, e * LC + tt * CG: e * LC + tt * CG + CG]
        ln_rows(b, Xt, persist_base=32 * tt)
    fence(b)

    import os as _os_a
    trunc = int(_os_a.environ.get("ATTNA_TRUNC", "99"))
    if trunc <= 1:
        return
    for pss in range(2):
        if pss == 1:
            if trunc <= 4:
                return
            SY.dma_start(out=t["cc_a_in"][:, 0:16], in_=t["RV"][:, :]).inc(ld, 16)
            SY.dma_start(out=t["cc_a_in"][:, 16:32], in_=t["RV2"][:, :]).inc(ld, 16)
            SY.dma_start(out=t["cc_a_in"][:, 32:544], in_=t["KVR"][:, :]).inc(ld, 16)
            SY.dma_start(out=t["cc_a_in"][0:16, 544:552], in_=t["RSC"][:, :]).inc(ld, 16)
            GP.wait(ld, ld.c)
            ccv = GP.collective_compute("AllGather", ALU.bypass,
                                        replica_groups=[[0, 1], [2, 3], [4, 5], [6, 7]],
                                        ins=[t["cc_a_in"].opt()],
                                        outs=[t["cc_a_out"].opt()]).inc(cc, 1)
            b.a_ccv = ccv
            if trunc <= 5:
                return

        for cg in range(NTT):
            if pss == 0 and cg >= 1 and trunc <= 3:
                break
            if pss == 1 and cg >= trunc - 5 and trunc <= 8:
                return
            if pss == 1 and trunc == 9 and cg in (1, 2):
                continue
            Xt = lambda e, cg=cg: t["X"][:, e * LC + cg * CG: e * LC + cg * CG + CG]
            XNe = lambda e: t["XN"][:, e * CG:(e + 1) * CG]
            bcv = ln_bcast(b, b.sems["ve"].c, persist_base=32 * cg)
            xnv = apply_ln(b, Xt, XNe, bcv)
            TE.wait(ve, xnv)
            direct = (pss == 0 and cg == 0)
            for wname, feat_dst, tok_dst, bias_c, bias_r, sig in (
                    ("a_Wq", "QF", QT, "a_bq_c_s", "a_bq_r", True),
                    ("a_Wk", "KF", KT, "a_bk_c_s", "a_bk_r", True),
                    ("a_Wv", None, VT, "a_bv_c_s", "a_bv_r", False)):
                wsrc = t[wname] if direct else t[wname + "_bf"]
                wready = None if direct else b.cast_ready[wname]
                widxs = [ws2.load(wsrc[:, q4 * 256:(q4 + 1) * 256],
                                  rearr=("(c p) f -> p c f", dict(p=128)), inner=256,
                                  ready=wready) for q4 in range(4)]
                if feat_dst is not None:
                    fd = t[feat_dst]
                    proj_feat(b, ws2, wname, XNe,
                              (lambda ob, fd=fd: fd[:, ob * CG:(ob + 1) * CG],),
                              bias_col=bias_c, sigmoid=sig, banks=("p0", "p1"),
                              widxs=widxs, release=False)
                proj_tok(b, [w[1] for w in widxs], [w[0] for w in widxs], ws2, XNe,
                         tok_dst, bias_row=bias_r, sigmoid=sig, banks=("p0", "p1"))
            fence(b)
            if pss == 1 and cg == 0:
                GP.wait(cc, b.a_ccv)
                GP.dma_start(out=t["RV"][:, :], in_=t["cc_a_out"][0:128, 0:16]).inc(gp, 16)
                GP.dma_start(out=t["RV2"][:, :], in_=t["cc_a_out"][0:128, 16:32]).inc(gp, 16)
                GP.dma_start(out=t["KVR"][:, :], in_=t["cc_a_out"][0:128, 32:544]).inc(gp, 16)
                GP.dma_start(out=t["RSC"][:, :], in_=t["cc_a_out"][0:16, 544:552]).inc(gp, 16)
                VE.wait(gp, gp.c)
                VE.tensor_scalar_mul(t["RV"][:, :], t["RV"][:, :], t["role_mask_s"][:, 0:1])
                VE.tensor_scalar_mul(t["RV2"][:, :], t["RV2"][:, :], t["role_mask_s"][:, 0:1])
                VE.tensor_scalar_mul(t["KVR"][:, :], t["KVR"][:, :], t["role_mask_s"][:, 0:1])
                VE.tensor_scalar_mul(t["RSC"][:, :], t["RSC"][:, :], t["role_mask_s"][0:16, 0:1])
            if pss == 0 and cg == 0:
                cast_phase(b, 0)
            if pss == 0 and cg < 3:
                cast_phase(b, cg + 1)
            if trunc <= 2:
                return
            for c4 in range(4):
                mixer_chunk(b, pss, cg, c4)
            if pss == 1:
                attn_a_tail(b, cg, ws2)
            fence(b)


def attn_a_tail(b, cg, ws2):
    t = b.t
    TE, VE, SE = b.TE, b.VE, b.SE
    te, ve, se = b.sems["te"], b.sems["ve"], b.sems["se"]
    Xt = lambda e: t["X"][:, e * LC + cg * CG: e * LC + cg * CG + CG]
    XNe = lambda e: t["XN"][:, e * CG:(e + 1) * CG]
    MIXe = lambda e: t["MIX"][:, e * CG:(e + 1) * CG]
    SC0e = lambda e: t["SC0"][:, e * CG:(e + 1) * CG]

    def wo_out(ob, bank, mmv):
        bname = "p0" if ob % 2 == 0 else "p1"
        s1sl = t["SC1"][:, (ob % 2) * 512:(ob % 2) * 512 + 512]
        VE.wait(te, mmv)
        VE.tensor_mul(s1sl, Xt(ob), t["BC"][:, 0:512])
        VE.tensor_add(s1sl, s1sl, t["BC"][:, 512:1024])
        xv = VE.scalar_tensor_tensor(SC0e(ob), bank[:, :], t["a_bo_c_s"][:, ob:ob + 1],
                                     s1sl, ALU.add, ALU.add).inc(ve)
        b.bank_rel[bname] = (b.sems["ve"], xv)

    proj_feat(b, ws2, "a_Wo", MIXe, wo_out, banks=("p0", "p1"))
    fence(b)
    rv = ln_rows(b, SC0e)
    bcv = ln_bcast(b, rv)
    hv = apply_ln(b, SC0e, SC0e, bcv, gcol="a_ln1_g_c_s", bcol="a_ln1_b_c_s")
    VE.wait(ve, hv)
    for e in range(8):
        hbv = VE.tensor_copy(XNe(e), SC0e(e)).inc(ve)
    act_v = None
    for hb2 in range(16):
        idx, wdst = ws2.load(t["a_W1_bf"][:, hb2 * 256:(hb2 + 1) * 256],
                             rearr=("(c p) h -> p c h", dict(p=128)), inner=256,
                             ready=b.cast_ready["a_W1"])
        for sub in range(2):
            hb = hb2 * 2 + sub
            bname = ("p0", "p1", "p2", "p3")[hb % 4]
            bank = t[bname]
            bank_wait(b, TE, bname)
            if sub == 0:
                ws2.wait_on(TE, idx)
            if hb == 0:
                TE.wait(ve, hbv)
            for e in range(8):
                op = TE.matmul(bank[:, :],
                               wdst[:, e * 256 + sub * 128: e * 256 + sub * 128 + 128],
                               XNe(e), start=(e == 0), stop=(e == 7))
            mmv = op.inc(b.sems["te"])
            if sub == 1:
                ws2.release(idx, b.sems["te"], mmv)
            SE.wait(te, mmv)
            av = SE.activation(t["ACT"][:, hb * 512:(hb + 1) * 512], bank[:, :], AF.Gelu,
                               bias=t["a_b1_c_s"][:, hb:hb + 1]).inc(b.sems["se"])
            b.bank_rel[bname] = (b.sems["se"], av)
            act_v = av
    for ob in range(8):
        bname = "p2" if ob % 2 == 0 else "p3"
        bank = t[bname]
        for half in range(2):
            idx, wdst = ws2.load(
                t["a_W2_bf"][half * 2048:(half + 1) * 2048, ob * 128:(ob + 1) * 128],
                rearr=("(c p) d -> p c d", dict(p=128)), inner=128,
                ready=b.cast_ready["a_W2"])
            if half == 0:
                bank_wait(b, TE, bname)
                TE.wait(se, act_v)
            ws2.wait_on(TE, idx)
            for c16 in range(16):
                hb = half * 16 + c16
                op = TE.matmul(bank[:, :], wdst[:, c16 * 128:(c16 + 1) * 128],
                               t["ACT"][:, hb * 512:(hb + 1) * 512],
                               start=(hb == 0), stop=(hb == 31))
            mmv = op.inc(b.sems["te"])
            ws2.release(idx, b.sems["te"], mmv)
        VE.wait(te, mmv)
        uv = VE.scalar_tensor_tensor(SC0e(ob), bank[:, :], t["a_b2_c_s"][:, ob:ob + 1],
                                     SC0e(ob), ALU.add, ALU.add).inc(b.sems["ve"])
        b.bank_rel[bname] = (b.sems["ve"], uv)
    fence(b)
    rv = ln_rows(b, SC0e)
    bcv = ln_bcast(b, rv)
    VE.wait(ve, bcv)
    for e in range(8):
        s1sl = t["SC1"][:, (e % 2) * 512:(e % 2) * 512 + 512]
        VE.tensor_mul(s1sl, SC0e(e), t["BC"][:, 0:512])
        VE.tensor_add(s1sl, s1sl, t["BC"][:, 512:1024])
        VE.tensor_scalar(s1sl, s1sl, t["a_ln2_g_c_s"][:, e:e + 1], t["a_ln2_b_c_s"][:, e:e + 1],
                         ALU.mult, ALU.add)
        VE.tensor_add(Xt(e), Xt(e), s1sl)


def emit_lntest(b):
    t = b.t
    for tt in range(NTT):
        Xt = lambda e, tt=tt: t["X"][:, e * LC + tt * CG: e * LC + tt * CG + CG]
        rv = ln_rows(b, Xt)
        bcv = ln_bcast(b, rv)
        apply_ln(b, Xt, Xt, bcv)
        fence(b)


def emit_qctest(b):
    t = b.t
    b.VE.tensor_copy(t["X"][:, 0:32], t["f_c1"][:, :])
    b.VE.tensor_copy(t["X"][:, 32:64], t["f_c2"][:, :])
    b.VE.tensor_copy(t["X"][:, 64:96], t["COLS"][:, 0:32])
    b.VE.tensor_copy(t["X"][:, 96:128], t["COLS"][:, 32:64])
    b.VE.tensor_copy(t["X"][:, 128:160], t["COLS"][:, 64:96])
    b.VE.tensor_copy(t["X"][:, 160:192], t["COLS"][:, 96:128])
    b.VE.tensor_copy(t["X"][:, 192:224], t["COLS2"][:, 0:32])
    fence(b)


def epilogue_out(b):
    t = b.t
    fence(b)
    ld = b.sems["ld"]
    for e in range(8):
        b.SY.dma_start(out=t["xout"][e * 128:(e + 1) * 128, :],
                       in_=t["X"][:, e * LC:(e + 1) * LC]).inc(ld, 16)
    b.GP.wait(ld, ld.c)


def dump_dbg(b, i):
    t = b.t
    fence(b)
    ld = b.sems["ld"]
    for e in range(8):
        sdma(b, b.SY, ld, t[f"dbg{i}"][e * 128:(e + 1) * 128, :], t["X"][:, e * LC:(e + 1) * LC])
    fence(b)


_PHASES_ALL = ("attn_a", "ffn1", "attn_b", "ffn2")


def build_program(phases=_PHASES_ALL, dbg=0, dbg_rows=False):
    nc = bass.Bass("TRN2", target_bir_lowering=False, debug=False)
    b = B(nc)
    b.dbg_rows = dbg_rows
    import os as _os
    b.dbg_fence = int(_os.environ.get('DBGFENCE', '0'))
    declare_io(nc, b.t, dbg=dbg)
    with ExitStack() as ctx:
        alloc_sbuf(b, ctx)
        for name in ("ld", "gp", "te", "ve", "se", "cc",
                     "w1s0", "w1s1", "w2s0", "w2s1", "w2s2", "w2s3",
                     "kv0", "kv1", "mb0", "mb1", "cw0", "cw1", "cw2", "cw3"):
            b.sem(ctx, name)
        b.bank_rel = {}
        b.flash_rel = {}
        prologue(b)
        quantum_consts(b)
        ndbg = 0
        for ph in phases:
            if ph == "attn_a":
                emit_attn_a(b)
            elif ph in ("ffn1", "ffn2"):
                emit_ffn(b, "f")
            elif ph == "attn_b":
                emit_attn_b(b)
            elif ph == "dbg":
                dump_dbg(b, ndbg)
                ndbg += 1
            elif ph == "lntest":
                emit_lntest(b)
            elif ph == "qctest":
                emit_qctest(b)
            else:
                raise ValueError(ph)
        epilogue_out(b)
        b.replay()
    return nc


import os as _os_k


def kernel(**inputs):
    ph = _os_k.environ.get("KPHASES", "")
    nc = build_program(phases=tuple(ph.split(",")) if ph else _PHASES_ALL)
    in_maps = [make_host_inputs(inputs, c) for c in range(8)]
    res = run_bass_kernel_spmd(nc, in_maps, core_ids=list(range(8)))
    out = np.zeros((4, 4096, 1024), np.float32)
    for c in range(8):
        b_, r = c // 2, c % 2
        out[b_, r * LC:(r + 1) * LC, :] = np.asarray(res.results[c]["xout"]).T
    return out
